# version 13
# speedup vs baseline: 1.1960x; 1.0011x over previous
import math
from contextlib import ExitStack

import numpy as np
import ml_dtypes

import concourse.bass as bass
import concourse.mybir as mybir
from concourse.bass_utils import run_bass_kernel_spmd

F32 = mybir.dt.float32
BF16 = mybir.dt.bfloat16
I32 = mybir.dt.int32
AF = mybir.ActivationFunctionType
ALU = mybir.AluOpType
AX = mybir.AxisListType

NCORES = 8
B = 4
L = 4096
D = 1024
HL = L // 2
NCH = D // 128
EPS = 1e-5
NH = 16
DH = 64
NEGBIG = -60000.0

bf16 = ml_dtypes.bfloat16



class RecCall:
    __slots__ = ("post",)

    def __init__(self):
        self.post = []

    def then_inc(self, sem, n=1):
        self.post.append((sem, n))
        return self


class RecEngine:
    def __init__(self, prog, name):
        self._prog = prog
        self._name = name

    def __getattr__(self, meth):
        prog, name = self._prog, self._name

        def f(*a, **kw):
            rc = RecCall()
            prog.q[name].append((meth, a, kw, rc))
            return rc

        return f


class Sem:
    def __init__(self, handle):
        self.h = handle
        self.n = 0

    def inc(self, rc, k=1):
        rc.then_inc(self.h, k)
        self.n += k
        return self.n


ENGINES = ("tensor", "vector", "scalar", "gpsimd", "sync")


class Prog:
    def __init__(self, nc, stack):
        self.nc = nc
        self.stack = stack
        self.q = {e: [] for e in ENGINES}
        self.t = RecEngine(self, "tensor")
        self.v = RecEngine(self, "vector")
        self.s = RecEngine(self, "scalar")
        self.g = RecEngine(self, "gpsimd")
        self.y = RecEngine(self, "sync")
        self._id = 0
        self.pe = self.sem("s_pe")
        self.ve = self.sem("s_ve")
        self.se = self.sem("s_se")
        self.ld = self.sem("s_ld")
        self.gd = self.sem("s_gd")
        self.st = self.sem("s_st")
        self._fscr = None
        self._fscr2 = None

    def init_fence(self):
        self._fscr = self.sb([1, 1], F32, "fence_src")
        self._fscr2 = self.sb([1, 1], F32, "fence_dst")
        self.v.memset(self._fscr[:, :], 0.0)
        self.ve.inc(self.last("vector"))

    def _nm(self, pfx):
        self._id += 1
        return f"{pfx}{self._id}"

    def _mangle(self, name):
        sid = getattr(self, "_scope_id", None)
        if sid is not None and name is not None:
            return f"g{sid}_{name}"
        return name

    def sem(self, name=None):
        if name is not None:
            cache = getattr(self, "_semcache", None)
            if cache is None:
                cache = self._semcache = {}
            if name in cache:
                return cache[name]
            sm = Sem(self.stack.enter_context(self.nc.semaphore(name)))
            cache[name] = sm
            return sm
        return Sem(self.stack.enter_context(
            self.nc.semaphore(self._nm("sem"))))

    def sb(self, shape, dt, name=None):
        st = getattr(self, "_scope", None) or self.stack
        return st.enter_context(
            self.nc.sbuf_tensor(self._mangle(name) or self._nm("sb"),
                                shape, dt))

    def ps(self, shape, dt=F32, name=None):
        st = getattr(self, "_scope", None) or self.stack
        return st.enter_context(
            self.nc.psum_tensor(self._mangle(name) or self._nm("ps"),
                                shape, dt))

    def push_scope(self):
        self._scope = ExitStack()
        self._scope_id = getattr(self, "_scope_ctr", 0)
        self._scope_ctr = self._scope_id + 1
        return self._scope

    def pop_scope(self):
        self._scope.close()
        self._scope = None
        self._scope_id = None

    def stage_barrier(self):
        sems = [self.pe, self.ve, self.se, self.ld, self.gd, self.st]
        if getattr(self, "ccs", None) is not None:
            sems.append(self.ccs)
        for eng in ("tensor", "vector", "scalar", "gpsimd", "sync"):
            self.wait_all(eng, *sems)

    def last(self, eng):
        return self.q[eng][-1][3]

    def fence_sync(self, fence_sem):
        self.y.wait_ge(self.st.h, self.st.n)
        self.y.dma_start(out=self._fscr2[0:1, 0:1], in_=self._fscr[0:1, 0:1])
        return fence_sem.inc(self.last("sync"), 16)

    def mark(self, eng):
        m = {"tensor": self.pe, "vector": self.ve, "scalar": self.se}
        sem = m[eng]
        return sem.inc(self.last(eng))

    def wait_all(self, eng, *sems):
        e = getattr(self, {"tensor": "t", "vector": "v", "scalar": "s",
                           "gpsimd": "g", "sync": "y"}[eng])
        memo = getattr(self, "_wmemo", None)
        if memo is None:
            memo = self._wmemo = {}
        em = memo.setdefault(eng, {})
        for sm in sems:
            if sm.n > 0 and em.get(id(sm), -1) < sm.n:
                e.wait_ge(sm.h, sm.n)
                em[id(sm)] = sm.n

    def replay(self, block):
        selfsems = {e: self.sem(f"self_{e}")
                    for e in ("vector", "scalar", "gpsimd")}
        SKIP = {"dma_start", "dma_start_transpose", "wait_ge",
                "engine_nop", "collective_compute", "reg_load", "iota",
                "memzero"}
        totals = {}

        def ap_region(a):
            try:
                nm = a.tensor.name
                ap = a.ap
                off = a.offset
                if not isinstance(off, int):
                    return None
                span = 1
                for st_, ct in ap:
                    span += abs(int(st_)) * (int(ct) - 1)
                return (nm, off, off + span)
            except Exception:
                return None

        def mk(engname):
            ss = selfsems.get(engname)

            def runner(eng):
                idx = 0
                touch = {}
                covered = {}
                for meth, a, kw, rc in self.q[engname]:
                    guard = ss is not None and meth not in SKIP
                    deps = {}
                    regs_w, regs_r = [], []
                    if guard:
                        aps = [x for x in a if hasattr(x, "tensor")]
                        aps += [x for x in kw.values()
                                if hasattr(x, "tensor")]
                        for i2, x in enumerate(aps):
                            r = ap_region(x)
                            if r is None:
                                continue
                            (regs_w if i2 == 0 else regs_r).append(r)
                        if "accum_out" in kw and hasattr(kw["accum_out"],
                                                         "tensor"):
                            rr = ap_region(kw["accum_out"])
                            if rr:
                                regs_w.append(rr)
                        for nm, lo, hi, in [(q[0], q[1], q[2])
                                            for q in regs_w + regs_r]:
                            pass
                        def scan_conflicts(regs, want_writes_only):
                            for nm, lo, hi in regs:
                                for (hdl, l2, h2, isw) in touch.get(nm, ()):
                                    if l2 < hi and lo < h2 and (
                                            isw or not want_writes_only):
                                        key = id(hdl[0])
                                        if deps.get(key, (None, -1))[1] < hdl[1]:
                                            deps[key] = (hdl[0], hdl[1])
                        scan_conflicts(regs_r, True)
                        scan_conflicts(regs_w, False)
                    for key, (sem, val) in deps.items():
                        if covered.get(key, 0) < val:
                            eng.wait_ge(sem, val)
                            covered[key] = val
                    r = getattr(eng, meth)(*a, **kw)
                    if rc.post:
                        for sem, n in rc.post:
                            r.then_inc(sem, n)
                            totals[id(sem)] = totals.get(id(sem), 0) + n
                        hdl = (rc.post[0][0], totals[id(rc.post[0][0])])
                    elif guard:
                        idx += 1
                        r.then_inc(ss.h, 1)
                        hdl = (ss.h, idx)
                    else:
                        hdl = None
                    if guard and hdl is not None:
                        for nm, lo, hi in regs_w:
                            touch.setdefault(nm, []).append(
                                (hdl, lo, hi, True))
                            if len(touch[nm]) > 48:
                                touch[nm] = touch[nm][-48:]
                        for nm, lo, hi in regs_r:
                            touch.setdefault(nm, []).append(
                                (hdl, lo, hi, False))
                            if len(touch[nm]) > 48:
                                touch[nm] = touch[nm][-48:]
            return runner
        block.tensor(mk("tensor"))
        block.vector(mk("vector"))
        block.scalar(mk("scalar"))
        block.gpsimd(mk("gpsimd"))
        block.sync(mk("sync"))



class Ctx:
    def __init__(self, p):
        self.p = p
        self.ones_bf = p.sb([128, 2], BF16, "ones_bf")
        self.ones1_bf = p.sb([1, 128], BF16, "ones1_bf")
        self.ones1_f = p.sb([1, 128], F32, "ones1_f")
        self.eps1 = p.sb([1, 1], F32, "eps1")
        self.sq = p.sb([128, 1024], BF16, "sq_scr")
        self.srow = [p.sb([1, 512], F32, f"srow{i}") for i in range(4)]
        self.brow = [p.sb([1, 512], BF16, f"brow{i}") for i in range(2)]
        self._lncall = 0
        p.v.memset(self.ones_bf[:, :], 1.0)
        p.v.memset(self.ones1_bf[:, :], 1.0)
        p.v.memset(self.ones1_f[:, :], 1.0)
        p.v.memset(self.eps1[:, :], EPS)
        p.mark("vector")

    def ln_block(self, chunks, T, out_fn, gb=None):
        p = self.p
        t, v, s = p.t, p.v, p.s
        self._lncall += 1
        ln_stack = ExitStack()
        self.ps_sum = ln_stack.enter_context(
            p.nc.psum_tensor(f"ps_sum{self._lncall}", [1, 512], F32))
        self.ps_sq = ln_stack.enter_context(
            p.nc.psum_tensor(f"ps_sq{self._lncall}", [1, 512], F32))
        self.ps_bc = ln_stack.enter_context(
            p.nc.psum_tensor(f"ps_bc{self._lncall}", [128, 1024], F32))
        p.wait_all("scalar", p.ve, p.pe)
        for c in range(8):
            sqs = self.sq[:, (c % 2) * 512:(c % 2) * 512 + T]
            if c >= 2:
                p.wait_all("scalar", p.pe)
            s.activation(sqs, chunks[c], AF.Square)
            p.mark("scalar")
            p.wait_all("tensor", p.se)
            t.matmul(self.ps_sum[:, :T], self.ones_bf[:, 0:1], chunks[c],
                     start=(c == 0), stop=(c == 7))
            t.matmul(self.ps_sq[:, :T], self.ones_bf[:, 1:2], sqs,
                     start=(c == 0), stop=(c == 7))
            p.mark("tensor")
        mean, rstd, var, msq = (r[:, :T] for r in self.srow)
        p.wait_all("vector", p.pe)
        v.tensor_single_scalar(mean, self.ps_sum[:, :T], 1.0 / D, ALU.mult)
        v.tensor_mul(msq, mean, mean)
        v.scalar_tensor_tensor(var, self.ps_sq[:, :T], 1.0 / D, msq,
                               ALU.mult, ALU.subtract)
        v.tensor_copy(self.brow[0][:, :T], mean)
        p.mark("vector")
        p.wait_all("scalar", p.ve)
        s.activation(var, var, AF.Ln, bias=self.eps1[:, 0:1])
        s.activation(rstd, var, AF.Exp, scale=-0.5)
        p.mark("scalar")
        p.wait_all("vector", p.se)
        v.tensor_copy(self.brow[1][:, :T], rstd)
        p.mark("vector")
        p.wait_all("tensor", p.ve)
        t.matmul(self.ps_bc[:, 0:T], self.ones1_bf[:, :], self.brow[0][:, :T],
                 start=True, stop=True)
        t.matmul(self.ps_bc[:, 512:512 + T], self.ones1_bf[:, :],
                 self.brow[1][:, :T], start=True, stop=True)
        p.mark("tensor")
        p.wait_all("vector", p.pe)
        mb = self.ps_bc[:, 0:T]
        rb = self.ps_bc[:, 512:512 + T]
        for c in range(8):
            out_fn(c, chunks[c], mb, rb)
        if not p.q["vector"][-1][3].post:
            p.mark("vector")
        ln_stack.close()



def emit_l0_attn(p, cx, j, x_h, wq_h, wk_h, wv_h, wo_h, dmask_h,
                 xres1_h, xin0_h, khbm, krow_h=None):
    t0 = j * HL
    LK = (j + 1) * HL
    NBLK = LK // 512
    OWNB = NBLK - 4
    scale_q = float(DH) ** -0.5

    t, v, s, g, y = p.t, p.v, p.s, p.g, p.y
    p.init_fence()
    wl = p.sem("wl")
    xld = [p.sem("xld0"), p.sem("xld1")]
    kevfree = p.sem("kevfree")

    VW = NH * 65
    avhbm = p.nc.dram_tensor("avhbm", [4, NCH, 128, 512], BF16).ap()
    kmax_all = p.sb([128, NCH * 8], F32, "kmax_all")
    kmaxc = p.sb([128, NCH], F32, "kmaxc")
    kmbt = p.sb([64, NH], BF16, "kmbt")
    dmask = p.sb([128, 128], F32, "dmask_sb")
    sc_q = p.sb([128, 1], F32, "sc_q")
    ones65 = p.sb([1, L], BF16, "ones65")

    v.memset(sc_q[:, :], scale_q)
    v.memset(kmax_all[:, :], 0.0)
    v.memset(ones65[:, :], 1.0)
    p.mark("vector")
    y.dma_start(out=dmask[:, :], in_=dmask_h)
    dm_ld = p.ld.inc(p.last("sync"), 16)

    p.wait_all("sync", p.ve)
    for h in range(NH):
        if krow_h is not None:
            g.dma_start(out=khbm[h, 64:65, :LK], in_=krow_h[:, :LK])
            p.st.inc(p.last("gpsimd"), 16)
        else:
            y.dma_start(out=khbm[h, 64:65, :LK], in_=ones65[:, :LK])
            p.st.inc(p.last("sync"), 16)

    stackAB = ExitStack()
    vtm = stackAB.enter_context(
        p.nc.sbuf_tensor("vtm", [128, (LK // 128) * VW], BF16))
    qfm = [stackAB.enter_context(p.nc.sbuf_tensor(f"qfm{c}", [128, HL], BF16))
           for c in range(NCH)]
    vt_view = vtm[:, :].rearrange("p (n h e) -> p n h e", h=NH, e=65)
    v.memset(vt_view[:, :, :, 64:65], 1.0)
    p.mark("vector")
    with ExitStack() as pha:
        xblk = [pha.enter_context(p.nc.sbuf_tensor(f"xblk{i}", [128, 8 * 512], BF16))
                for i in range(2)]
        wq_sb = pha.enter_context(p.nc.sbuf_tensor("wq_sb", [128, 8 * D], BF16))
        wk_sb = pha.enter_context(p.nc.sbuf_tensor("wk_sb", [128, 8 * D], BF16))
        wv_sb = pha.enter_context(p.nc.sbuf_tensor("wv_sb", [128, 8 * D], BF16))
        kev = pha.enter_context(p.nc.sbuf_tensor("kev", [128, 8 * 512], BF16))
        ps_k = pha.enter_context(p.nc.psum_tensor("ps_k", [128, 512], F32))
        ps_v = pha.enter_context(p.nc.psum_tensor("ps_v", [128, 512], F32))

        for c in range(NCH):
            g.dma_start(out=wq_sb[:, c * D:(c + 1) * D],
                        in_=wq_h[c * 128:(c + 1) * 128, :])
            wl.inc(p.last("gpsimd"), 16)
            g.dma_start(out=wk_sb[:, c * D:(c + 1) * D],
                        in_=wk_h[c * 128:(c + 1) * 128, :])
            wl.inc(p.last("gpsimd"), 16)
            g.dma_start(out=wv_sb[:, c * D:(c + 1) * D],
                        in_=wv_h[c * 128:(c + 1) * 128, :])
            wl.inc(p.last("gpsimd"), 16)
        wl_done = wl.n

        blk_consumed_pe = [0, 0]
        for blk in range(NBLK):
            buf = xblk[blk % 2]
            bufc = [buf[:, c * 512:(c + 1) * 512] for c in range(NCH)]
            if blk >= 2:
                g.wait_ge(p.pe.h, blk_consumed_pe[blk % 2])
            g.dma_start(
                out=buf[:, :].rearrange("p (c q) -> p c q", c=NCH),
                in_=x_h[:, blk * 512:(blk + 1) * 512].rearrange(
                    "(c p) q -> p c q", p=128))
            xld[blk % 2].inc(p.last("gpsimd"), 16)
            p.wait_all("scalar", xld[blk % 2])
            p.wait_all("tensor", xld[blk % 2])
            p.wait_all("vector", xld[blk % 2])

            def ln_out(c, ch, mb, rb):
                v.tensor_sub(ch, ch, mb)
                v.tensor_mul(ch, ch, rb)
            cx.ln_block(bufc, 512, ln_out)

            p.wait_all("tensor", p.ve)
            if blk == 0:
                t.wait_ge(wl.h, wl_done)
            for oc in range(NCH):
                p.wait_all("tensor", p.se)
                for c in range(NCH):
                    t.matmul(ps_k[:, :],
                             wk_sb[:, c * D + oc * 128: c * D + oc * 128 + 128],
                             bufc[c], start=(c == 0), stop=(c == NCH - 1))
                pe_k = p.mark("tensor")
                slot = kev[:, oc * 512:oc * 512 + 512]
                s.wait_ge(p.pe.h, pe_k)
                if blk >= 1:
                    s.wait_ge(kevfree.h, 16 * blk)
                s.activation(slot, ps_k[:, :], AF.Copy)
                se_k = p.mark("scalar")
                v.wait_ge(p.se.h, se_k)
                v.reduce_max(kmax_all[:, oc * 8 + blk:oc * 8 + blk + 1], slot,
                             axis=AX.X, apply_absolute_value=True)
                p.mark("vector")
                y.wait_ge(p.se.h, se_k)
                y.dma_start(out=khbm[2 * oc, 0:64, blk * 512:(blk + 1) * 512],
                            in_=slot[0:64, :])
                p.st.inc(p.last("sync"), 16)
                y.dma_start(out=khbm[2 * oc + 1, 0:64,
                                     blk * 512:(blk + 1) * 512],
                            in_=slot[64:128, :])
                p.st.inc(p.last("sync"), 16)

            p.fence_sync(kevfree)

            for sub in range(4):
                tok = blk * 4 + sub
                for dvg in range(2):
                    p.wait_all("tensor", p.ve)
                    for c in range(NCH):
                        t.matmul(ps_v[:, :], bufc[c][:, sub * 128:(sub + 1) * 128],
                                 wv_sb[:, c * D + dvg * 512:c * D + dvg * 512 + 512],
                                 start=(c == 0), stop=(c == NCH - 1))
                    pe_v = p.mark("tensor")
                    v.wait_ge(p.pe.h, pe_v)
                    dst = vtm[:, tok * VW + dvg * 8 * 65:].rearrange(
                        "p (h e) -> p h e", h=8 + 8 * 0, e=65) \
                        if False else None
                    base = tok * VW + dvg * 8 * 65
                    dstv = bass.AP(vtm, 0, [[1, 128], [65, 8], [1, 64]])
                    dstv = vtm[:, base:base + 8 * 65].rearrange(
                        "p (h e) -> p h e", h=8)[:, :, 0:64]
                    v.tensor_copy(dstv, ps_v[:, :].rearrange(
                        "p (h e) -> p h e", h=8))
                    p.mark("vector")

            if blk >= OWNB:
                rel = blk - OWNB
                for oc in range(NCH):
                    p.wait_all("tensor", p.se)
                    for c in range(NCH):
                        t.matmul(ps_k[:, :],
                                 wq_sb[:, c * D + oc * 128:c * D + oc * 128 + 128],
                                 bufc[c], start=(c == 0), stop=(c == NCH - 1))
                    pe_q = p.mark("tensor")
                    s.wait_ge(p.pe.h, pe_q)
                    s.activation(qfm[oc][:, rel * 512:(rel + 1) * 512],
                                 ps_k[:, :], AF.Copy, scale=sc_q[:, 0:1])
                    p.mark("scalar")
            blk_consumed_pe[blk % 2] = p.pe.n

        p.wait_all("vector", p.ve)
        km_view = kmax_all[:, :].rearrange("p (o k) -> p o k", o=NCH)
        v.reduce_max(kmaxc[:, :, None], km_view, axis=AX.X)
        v.tensor_single_scalar(kmaxc[:, :], kmaxc[:, :], -1.0, ALU.mult)
        for h in range(NH):
            v.tensor_copy(kmbt[:, h:h + 1],
                          kmaxc[(h % 2) * 64:(h % 2) * 64 + 64,
                                h // 2:h // 2 + 1])
        p.mark("vector")
        phaseA_pe, phaseA_ve, phaseA_se = p.pe.n, p.ve.n, p.se.n
        phaseA_st = p.st.n

    with ExitStack() as phb:
        kaug = [phb.enter_context(p.nc.sbuf_tensor(f"kaug{i}", [65, LK], BF16))
                for i in range(4)]
        qa = phb.enter_context(p.nc.sbuf_tensor("qa", [65, 512], BF16))
        qabs = phb.enter_context(p.nc.sbuf_tensor("qabs", [64, 512], BF16))
        et = phb.enter_context(p.nc.sbuf_tensor("et", [128, 2048], BF16))
        avt = phb.enter_context(p.nc.sbuf_tensor("avt", [64, 512], BF16))
        avt2 = phb.enter_context(p.nc.sbuf_tensor("avt2", [64, 512], BF16))
        rz = phb.enter_context(p.nc.sbuf_tensor("rz", [1, 512], F32))
        ps_st_ = phb.enter_context(p.nc.psum_tensor("ps_st", [128, 2048], F32))
        ps_av = phb.enter_context(p.nc.psum_tensor("ps_av", [65, 512], F32))
        ps_rz = phb.enter_context(p.nc.psum_tensor("ps_rz", [64, 512], F32))
        ps_bnd = phb.enter_context(p.nc.psum_tensor("ps_bnd", [1, 512], F32))

        kl = p.sem("kl")
        for half in range(4):
            p.wait_all("sync", p.pe)
            for i in range(4):
                h = half * 4 + i
                y.wait_ge(p.st.h, phaseA_st)
                y.dma_start(out=kaug[i][:, :], in_=khbm[h, :, 0:LK])
                kl.inc(p.last("sync"), 16)
            for i in range(4):
                h = half * 4 + i
                for qg in range(4):
                    p.wait_all("scalar", p.ve)
                    s.wait_ge(kl.h, kl.n)
                    s.activation(qabs[:, :],
                                 qfm[h // 2][(h % 2) * 64:(h % 2) * 64 + 64,
                                             qg * 512:(qg + 1) * 512],
                                 AF.Abs)
                    se_a = p.mark("scalar")
                    gq = OWNB + qg
                    nkc = (gq + 1) * 4
                    p.wait_all("vector", p.pe)
                    v.wait_ge(p.se.h, se_a)
                    v.tensor_copy(qa[0:64, :],
                                  qfm[h // 2][(h % 2) * 64:(h % 2) * 64 + 64,
                                              qg * 512:(qg + 1) * 512])
                    ve_qa = p.mark("vector")
                    t.wait_ge(p.ve.h, ve_qa)
                    t.matmul(ps_bnd[:, :], kmbt[:, h:h + 1],
                             qabs[:, :], start=True, stop=True)
                    pe_b = p.mark("tensor")
                    s.wait_ge(p.pe.h, pe_b)
                    s.activation(qa[64:65, :], ps_bnd[:, :], AF.Copy)
                    se_b = p.mark("scalar")
                    first_av = True
                    for kg in range(0, nkc, 4):
                        ngrp = min(4, nkc - kg)
                        t.wait_ge(p.se.h, se_b)
                        for r in range(ngrp):
                            kc = kg + r
                            t.matmul(ps_st_[:, r * 512:(r + 1) * 512],
                                     kaug[i][:, kc * 128:(kc + 1) * 128],
                                     qa[:, :], start=True, stop=True)
                        pe_s = p.mark("tensor")
                        diag0 = gq * 4
                        v.wait_ge(p.ld.h, dm_ld)
                        v.wait_ge(p.pe.h, pe_s)
                        ve_m = None
                        for r in range(ngrp):
                            kc = kg + r
                            if kc >= diag0:
                                rr = kc - diag0
                                v.tensor_add(
                                    ps_st_[:, r * 512 + rr * 128:
                                           r * 512 + rr * 128 + 128],
                                    ps_st_[:, r * 512 + rr * 128:
                                           r * 512 + rr * 128 + 128],
                                    dmask[:, :])
                                if rr > 0:
                                    v.memset(et[:, r * 512:r * 512 + rr * 128],
                                             0.0)
                        ve_m = p.mark("vector")
                        s.wait_ge(p.ve.h, ve_m)
                        for r in range(ngrp):
                            kc = kg + r
                            rr = kc - diag0
                            off = max(rr, 0) * 128 if kc >= diag0 else 0
                            s.activation(et[:, r * 512 + off:(r + 1) * 512],
                                         ps_st_[:, r * 512 + off:(r + 1) * 512],
                                         AF.Exp)
                        se_e = p.mark("scalar")
                        t.wait_ge(p.se.h, se_e)
                        for r in range(ngrp):
                            kc = kg + r
                            t.matmul(ps_av[:, :],
                                     vtm[:, kc * VW + h * 65:kc * VW + h * 65 + 65],
                                     et[:, r * 512:(r + 1) * 512],
                                     start=first_av,
                                     stop=(kg + ngrp >= nkc and r == ngrp - 1))
                            first_av = False
                        pe_av = p.mark("tensor")
                    v.wait_ge(p.pe.h, pe_av)
                    v.reciprocal(rz[:, :], ps_av[64:65, :])
                    ve_r = p.mark("vector")
                    t.wait_ge(p.ve.h, ve_r)
                    t.matmul(ps_rz[:, :], cx.ones1_f[:, 0:64], rz[:, :],
                             start=True, stop=True)
                    pe_r = p.mark("tensor")
                    s.wait_ge(p.pe.h, pe_r)
                    s.activation(avt[:, :], ps_av[0:64, :], AF.Copy)
                    se_av = p.mark("scalar")
                    v.wait_ge(p.se.h, se_av)
                    v.wait_ge(p.st.h, p.st.n)
                    v.tensor_mul(avt2[:, :], avt[:, :], ps_rz[:, :])
                    ve_av = p.mark("vector")
                    y.wait_ge(p.ve.h, ve_av)
                    y.dma_start(out=avhbm[qg, h // 2,
                                          (h % 2) * 64:(h % 2) * 64 + 64, :],
                                in_=avt2[:, :])
                    p.st.inc(p.last("sync"), 16)
        phaseB_pe, phaseB_ve = p.pe.n, p.ve.n
    stackAB.close()

    with ExitStack() as phc:
        wo_sb = phc.enter_context(p.nc.sbuf_tensor("wo_sb", [128, 8 * D], BF16))
        avq = phc.enter_context(p.nc.sbuf_tensor("avq", [128, 8 * 512], BF16))
        ao = phc.enter_context(p.nc.sbuf_tensor("ao", [128, 8 * 512], BF16))
        xres_t = phc.enter_context(
            p.nc.sbuf_tensor("xres_t", [128, 8 * 512], F32))
        res1 = phc.enter_context(p.nc.sbuf_tensor("res1", [128, 8 * 512], F32))
        res1b = phc.enter_context(
            p.nc.sbuf_tensor("res1b", [128, 8 * 512], BF16))
        tmp = phc.enter_context(p.nc.sbuf_tensor("tmpc", [128, 512], F32))
        ps_o = phc.enter_context(p.nc.psum_tensor("ps_o", [128, 512], F32))

        p.wait_all("gpsimd", p.ve)
        p.wait_all("gpsimd", p.se, p.pe)
        for c in range(NCH):
            g.dma_start(out=wo_sb[:, c * D:(c + 1) * D],
                        in_=wo_h[c * 128:(c + 1) * 128, :])
            p.gd.inc(p.last("gpsimd"), 16)

        for qg in range(4):
            y.wait_ge(p.ve.h, p.ve.n)
            y.wait_ge(p.pe.h, p.pe.n)
            y.wait_ge(p.st.h, p.st.n)
            y.dma_start(
                out=xres_t[:, :].rearrange("p (c q) -> p c q", c=NCH),
                in_=x_h[:, t0 + qg * 512:t0 + (qg + 1) * 512].rearrange(
                    "(c p) q -> p c q", p=128))
            ld_x = p.ld.inc(p.last("sync"), 16)
            y.dma_start(
                out=avq[:, :].rearrange("p (c q) -> p c q", c=NCH),
                in_=avhbm[qg].rearrange("c p q -> p c q"))
            ld_av = p.ld.inc(p.last("sync"), 16)

            p.wait_all("tensor", p.gd)
            for oc in range(NCH):
                p.wait_all("tensor", p.se)
                if oc == 0:
                    t.wait_ge(p.ld.h, ld_av)
                for ic in range(NCH):
                    t.matmul(ps_o[:, :],
                             wo_sb[:, ic * D + oc * 128:ic * D + oc * 128 + 128],
                             avq[:, ic * 512:(ic + 1) * 512],
                             start=(ic == 0), stop=(ic == NCH - 1))
                pe_o = p.mark("tensor")
                s.wait_ge(p.pe.h, pe_o)
                s.activation(ao[:, oc * 512:(oc + 1) * 512], ps_o[:, :],
                             AF.Copy)
                p.mark("scalar")
            p.wait_all("vector", p.se)
            p.wait_all("tensor", p.se)

            aoc = [ao[:, c * 512:(c + 1) * 512] for c in range(NCH)]
            v.wait_ge(p.ld.h, ld_av)

            def res_out(c, ch, mb, rb):
                tc = tmp[:, :]
                v.tensor_sub(tc, ch, mb)
                v.tensor_mul(tc, tc, rb)
                v.tensor_add(res1[:, c * 512:(c + 1) * 512], tc,
                             xres_t[:, c * 512:(c + 1) * 512])
                v.tensor_copy(res1b[:, c * 512:(c + 1) * 512],
                              res1[:, c * 512:(c + 1) * 512])
            cx.ln_block(aoc, 512, res_out)
            ve_r1 = p.ve.n
            y.wait_ge(p.ve.h, ve_r1)
            y.dma_start(
                out=xres1_h[:, qg * 512:(qg + 1) * 512].rearrange(
                    "(c p) q -> p c q", p=128),
                in_=res1[:, :].rearrange("p (c q) -> p c q", c=NCH))
            p.st.inc(p.last("sync"), 16)

            r1c = [res1b[:, c * 512:(c + 1) * 512] for c in range(NCH)]

            def xin_out(c, ch, mb, rb):
                sl = xres_t[:, c * 512:(c + 1) * 512]
                v.tensor_sub(sl, res1[:, c * 512:(c + 1) * 512], mb)
                v.tensor_mul(sl, sl, rb)
            cx.ln_block(r1c, 512, xin_out)
            ve_x0 = p.ve.n
            y.wait_ge(p.ve.h, ve_x0)
            y.dma_start(
                out=xin0_h[:, qg * 512:(qg + 1) * 512].rearrange(
                    "(c p) q -> p c q", p=128),
                in_=xres_t[:, :].rearrange("p (c q) -> p c q", c=NCH))
            p.st.inc(p.last("sync"), 16)
        y.wait_ge(p.st.h, p.st.n)


def build_l0_attn(j):
    nc = bass.Bass("TRN2", target_bir_lowering=False)
    x_h = nc.dram_tensor("x_fm", [D, L], F32, kind="ExternalInput").ap()
    wq_h = nc.dram_tensor("wq", [D, D], F32, kind="ExternalInput").ap()
    wk_h = nc.dram_tensor("wk", [D, D], F32, kind="ExternalInput").ap()
    wv_h = nc.dram_tensor("wv", [D, D], F32, kind="ExternalInput").ap()
    wo_h = nc.dram_tensor("wo", [D, D], F32, kind="ExternalInput").ap()
    dmask_h = nc.dram_tensor("dmask", [128, 128], F32, kind="ExternalInput").ap()
    xres1_h = nc.dram_tensor("xres1", [D, HL], F32, kind="ExternalOutput").ap()
    xin0_h = nc.dram_tensor("xin0", [D, HL], F32, kind="ExternalOutput").ap()
    khbm = nc.dram_tensor("khbm", [NH, 65, L], BF16).ap()

    with ExitStack() as stack:
        p = Prog(nc, stack)
        cx = Ctx(p)
        emit_l0_attn(p, cx, j, x_h, wq_h, wk_h, wv_h, wo_h, dmask_h,
                     xres1_h, xin0_h, khbm)
        block = stack.enter_context(nc.Block())
        p.replay(block)
    return nc



def _dmask_np():
    m = np.zeros((128, 128), dtype=np.float32)
    kk = np.arange(128)[:, None]
    qq = np.arange(128)[None, :]
    m[kk > qq] = NEGBIG
    return m


def stage_l0_attn_inputs(inputs):
    x = np.asarray(inputs["L0_attn_x"])
    maps = []
    dm = _dmask_np()
    for c in range(NCORES):
        b, j = divmod(c, 2)
        maps.append(dict(
            x_fm=np.ascontiguousarray(x[b].T),
            wq=np.asarray(inputs["L0_attn_Wq"]),
            wk=np.asarray(inputs["L0_attn_Wk"]),
            wv=np.asarray(inputs["L0_attn_Wv"]),
            wo=np.asarray(inputs["L0_attn_Wo"]),
            dmask=dm,
        ))
    return maps


def run_l0_attn(inputs):
    maps = stage_l0_attn_inputs(inputs)
    nc0 = build_l0_attn(0)
    nc1 = build_l0_attn(1)
    r0 = run_bass_kernel_spmd(nc0, [maps[2 * b] for b in range(B)],
                              core_ids=[0, 2, 4, 6])
    r1 = run_bass_kernel_spmd(nc1, [maps[2 * b + 1] for b in range(B)],
                              core_ids=[1, 3, 5, 7])
    return r0, r1


def kernel_unfused(**inputs):
    raise NotImplementedError("built incrementally; see test scripts")


HB = 260
W = 256


def emit_l0f_attn(p, cx, xin_h, wqkv_h, wo_h, qs_h, ks_h, tq1_h, tq2_h,
                  tk1_h, tk2_h, rt_h, hm_h, hmt_h, oh_h, amask_h, xout_h):
    t, v, s, g, y = p.t, p.v, p.s, p.g, p.y

    xinb = [p.sb([128, 8 * 257], F32, f"xinb{i}") for i in range(2)]
    xbf = p.sb([128, 8 * W], BF16, "xbf")
    qraw = p.sb([128, 8 * W], BF16, "qraw")
    q_t = p.sb([128, 8 * W], BF16, "q_t")
    qab = p.sb([128, 8 * W], BF16, "qab")
    k_t = [p.sb([128, 8 * W], BF16, f"k_t{i}") for i in range(2)]
    VW2 = NH * 65
    v_t = p.sb([128, 4 * VW2], BF16, "v_t")
    kmx = p.sb([128, 8 * 2], F32, "kmx")
    kmc = p.sb([128, 1], F32, "kmc")
    kme = p.sb([128, 8 * 16], BF16, "kme")
    sq2 = p.sb([128, W], BF16, "sq2")
    rowp = p.sb([16, W], F32, "rowp")
    rec = p.sb([16, W], BF16, "rec")
    bnd = p.sb([16, W], BF16, "bnd")
    et2 = p.sb([128, 4 * W], BF16, "et2")
    avt_ = p.sb([64, W], BF16, "avt_")
    av_all = p.sb([128, 8 * W], BF16, "av_all")
    rz2 = p.sb([1, W], F32, "rz2")
    xres2 = p.sb([128, W], F32, "xres2")
    e2s = p.sb([128, W], F32, "e2s")
    e3s = p.sb([128, W], F32, "e3s")
    wqkv_sb = p.sb([128, 8 * 3072], BF16, "wqkv_sb")
    wo_sb2 = p.sb([128, 8 * D], BF16, "wo_sb2")
    qs_sb = p.sb([128, 1], F32, "qs_sb")
    ks_sb = p.sb([128, 1], F32, "ks_sb")
    tq1 = p.sb([128, HB + HL], BF16, "tq1_sb")
    tq2 = p.sb([128, HB + HL], BF16, "tq2_sb")
    tk1 = p.sb([128, HB + HL], BF16, "tk1_sb")
    tk2 = p.sb([128, HB + HL], BF16, "tk2_sb")
    rt_sb = p.sb([128, 128], BF16, "rt_sb")
    hm_sb = p.sb([128, 8 * 16], BF16, "hm_sb")
    hmt_sb = p.sb([16, 8 * 128], BF16, "hmt_sb")
    oh_sb = p.sb([16, 16 * 128], BF16, "oh_sb")
    am_sb = p.sb([128, 2 * 4 * W], BF16, "am_sb")

    ps_p = p.ps([128, 512], F32, "ps_p")
    _pA = p.ps([128, 512], F32, "psA")
    ps_r = _pA[:, 0:W]
    ps_rb = _pA[:, W:2 * W]
    _pB = p.ps([16, 512], F32, "psB")
    ps_n = _pB[:, 0:W]
    ps_b2 = _pB[:, W:2 * W]
    ps_st2 = p.ps([128, 4 * W], F32, "ps_st2")
    _pC = p.ps([128, 512], F32, "psC")
    ps_av2 = _pC[0:65, 0:W]
    ps_rz2 = _pC[0:64, W:2 * W]
    ps_o2 = p.ps([128, W], F32, "ps_o2")

    wl = p.sem("wl2")
    xls = [p.sem("xls0"), p.sem("xls1")]

    for c in range(NCH):
        g.dma_start(out=wqkv_sb[:, c * 3072:(c + 1) * 3072],
                    in_=wqkv_h[c * 128:(c + 1) * 128, :])
        wl.inc(p.last("gpsimd"), 16)
        g.dma_start(out=wo_sb2[:, c * D:(c + 1) * D],
                    in_=wo_h[c * 128:(c + 1) * 128, :])
        wl.inc(p.last("gpsimd"), 16)
    for dst, src in ((tq1, tq1_h), (tq2, tq2_h), (tk1, tk1_h), (tk2, tk2_h),
                     (rt_sb, rt_h), (oh_sb, oh_h)):
        g.dma_start(out=dst[:, :], in_=src)
        wl.inc(p.last("gpsimd"), 16)
    g.dma_start(out=hm_sb[:, :].rearrange("p (c h) -> p c h", c=8),
                in_=hm_h.rearrange("c p h -> p c h"))
    wl.inc(p.last("gpsimd"), 16)
    g.dma_start(out=hmt_sb[:, :].rearrange("p (c q) -> p c q", c=8),
                in_=hmt_h.rearrange("c p q -> p c q"))
    wl.inc(p.last("gpsimd"), 16)
    g.dma_start(out=am_sb[:, :].rearrange("p (s k q) -> p s k q", s=2, k=4),
                in_=amask_h.rearrange("s k p q -> p s k q"))
    wl.inc(p.last("gpsimd"), 16)
    g.dma_start(out=qs_sb[:, :], in_=qs_h)
    wl.inc(p.last("gpsimd"), 16)
    g.dma_start(out=ks_sb[:, :], in_=ks_h)
    wl.inc(p.last("gpsimd"), 16)
    wl_total = wl.n

    vv = v_t[:, :].rearrange("p (n h e) -> p n h e", h=NH, e=65)
    v.memset(vv[:, :, :, 64:65], 1.0)
    p.mark("vector")

    xin_consumed = [0, 0]
    xin_consumed_ve = [0, 0]

    def produce(part, col0, scale_ap, t1_, t2_, dst, absdst, kslot):
        woff = 0 if part == "q" else D
        for oc in range(NCH):
            p.wait_all("tensor", p.se, p.ve)
            for c in range(NCH):
                t.matmul(ps_p[:, :W],
                         wqkv_sb[:, c * 3072 + woff + oc * 128:
                                 c * 3072 + woff + oc * 128 + 128],
                         xbf[:, c * W:(c + 1) * W],
                         start=(c == 0), stop=(c == NCH - 1))
            pe_pr = p.mark("tensor")
            s.wait_ge(p.pe.h, pe_pr)
            s.activation(sq2[:, :], ps_p[:, :W], AF.Square)
            s.activation(qraw[:, oc * W:(oc + 1) * W], ps_p[:, :W], AF.Copy,
                         scale=scale_ap[:, 0:1])
            se_ev = p.mark("scalar")
            t.wait_ge(p.se.h, se_ev)
            t.matmul(ps_n[:, :], hm_sb[:, oc * 16:(oc + 1) * 16],
                     sq2[:, :], start=(oc == 0), stop=(oc == NCH - 1))
            p.mark("tensor")
        p.wait_all("scalar", p.pe)
        s.activation(rowp[:, :], ps_n[:, :], AF.Sqrt)
        se_r = p.mark("scalar")
        v.wait_ge(p.se.h, se_r)
        v.tensor_scalar_max(rowp[:, :], rowp[:, :], 1e-12)
        v.reciprocal(rowp[:, :], rowp[:, :])
        v.tensor_copy(rec[:, :], rowp[:, :])
        p.mark("vector")
        for c in range(NCH):
            rsl = qraw[:, c * W:(c + 1) * W]
            p.wait_all("tensor", p.ve)
            t.matmul(ps_r[:, :], rt_sb[:, :], rsl, start=True, stop=True)
            t.matmul(ps_rb[:, :], hmt_sb[:, c * 128:(c + 1) * 128],
                     rec[:, :], start=True, stop=True)
            pe_rb = p.mark("tensor")
            v.wait_ge(p.pe.h, pe_rb)
            v.tensor_mul(e3s[:, :], rsl, t1_[:, col0:col0 + W])
            v.tensor_mul(e2s[:, :], ps_r[:, :], t2_[:, col0:col0 + W])
            v.tensor_add(e3s[:, :], e3s[:, :], e2s[:, :])
            v.tensor_mul(dst[:, c * W:(c + 1) * W], e3s[:, :], ps_rb[:, :])
            if absdst is not None:
                ve_d = p.mark("vector")
                s.wait_ge(p.ve.h, ve_d)
                s.activation(absdst[:, c * W:(c + 1) * W],
                             dst[:, c * W:(c + 1) * W], AF.Abs)
                p.mark("scalar")
            else:
                p.mark("vector")
        if kslot is not None:
            p.wait_all("vector", p.ve, p.se)
            for c in range(NCH):
                v.reduce_max(kmx[:, c * 2 + kslot:c * 2 + kslot + 1],
                             dst[:, c * W:(c + 1) * W],
                             axis=AX.X, apply_absolute_value=True)
            p.mark("vector")

    for bi in range(9):
        b = bi - 1
        slot = bi % 2
        col0 = HB + b * W
        buf = xinb[slot]
        if bi >= 2:
            g.wait_ge(p.pe.h, xin_consumed[slot])
            g.wait_ge(p.ve.h, xin_consumed_ve[slot])
        g.dma_start(
            out=buf[:, :].rearrange("p (c q) -> p c q", c=NCH),
            in_=xin_h[:, col0 - 1:col0 + W].rearrange("(c p) q -> p c q",
                                                      p=128))
        xls[slot].inc(p.last("gpsimd"), 16)
        p.wait_all("scalar", xls[slot])
        p.wait_all("scalar", p.pe, p.ve)
        g0 = p.gd
        if bi == 0:
            s.wait_ge(wl.h, wl_total)
        for c in range(NCH):
            off = 1 if c < 4 else 0
            s.activation(xbf[:, c * W:(c + 1) * W],
                         buf[:, c * 257 + off:c * 257 + off + W], AF.Copy)
        p.mark("scalar")
        p.wait_all("tensor", p.se)

        kst = bi % 2
        produce("k", col0, ks_sb, tk1, tk2, k_t[kst], None, kst)
        for sub in range(2):
            vtile = (bi * 2 + sub) % 4
            for dvg in range(2):
                p.wait_all("tensor", p.ve)
                for c in range(NCH):
                    t.matmul(ps_p[:, :],
                             xbf[:, c * W + sub * 128:c * W + sub * 128 + 128],
                             wqkv_sb[:, c * 3072 + 2 * D + dvg * 512:
                                     c * 3072 + 2 * D + dvg * 512 + 512],
                             start=(c == 0), stop=(c == NCH - 1))
                pe_v = p.mark("tensor")
                v.wait_ge(p.pe.h, pe_v)
                dstv = v_t[:, vtile * VW2 + dvg * 8 * 65:
                           vtile * VW2 + dvg * 8 * 65 + 8 * 65].rearrange(
                    "p (h e) -> p h e", h=8)[:, :, 0:64]
                v.tensor_copy(dstv, ps_p[:, :].rearrange("p (h e) -> p h e",
                                                         h=8))
                p.mark("vector")
        if b >= 0:
            produce("q", col0, qs_sb, tq1, tq2, q_t, qab, None)

            mset = 0 if b == 0 else 1
            p.wait_all("vector", p.ve, p.se)
            for c in range(NCH):
                v.tensor_max(kmc[:, :], kmx[:, c * 2:c * 2 + 1],
                             kmx[:, c * 2 + 1:c * 2 + 2])
                v.tensor_scalar(kme[:, c * 16:(c + 1) * 16],
                                hm_sb[:, c * 16:(c + 1) * 16],
                                kmc[:, 0:1], None, ALU.mult)
            ve_km = p.mark("vector")
            t.wait_ge(p.ve.h, ve_km)
            for c in range(NCH):
                t.matmul(ps_b2[:, :], kme[:, c * 16:(c + 1) * 16],
                         qab[:, c * W:(c + 1) * W],
                         start=(c == 0), stop=(c == NCH - 1))
            pe_b = p.mark("tensor")
            s.wait_ge(p.pe.h, pe_b)
            s.activation(bnd[:, :], ps_b2[:, :], AF.Copy, scale=-1.0)
            se_b = p.mark("scalar")

            kprev = (bi - 1) % 2
            kown = bi % 2
            for h in range(NH):
                par = h % 2
                c = h // 2
                base = par * 64
                p.wait_all("tensor", p.se, p.ve)
                for r in range(4):
                    t.matmul(ps_st2[:, r * W:(r + 1) * W],
                             oh_sb[:, h * 128:(h + 1) * 128],
                             bnd[:, :], start=(r % 2 == 0), stop=False)
                for r in range(4):
                    ksl = k_t[kprev if r < 2 else kown]
                    koff = c * W + (r % 2) * 128
                    t.matmul(ps_st2[:, r * W:(r + 1) * W],
                             ksl[base:base + 64, koff:koff + 128],
                             q_t[base:base + 64, c * W:(c + 1) * W],
                             start=False, stop=(r % 2 == 1))
                pe_s = p.mark("tensor")
                v.wait_ge(p.pe.h, pe_s)
                v.wait_ge(wl.h, wl_total)
                for r in range(4):
                    v.tensor_add(ps_st2[:, r * W:(r + 1) * W],
                                 ps_st2[:, r * W:(r + 1) * W],
                                 am_sb[:, mset * 4 * W + r * W:
                                       mset * 4 * W + (r + 1) * W])
                ve_m = p.mark("vector")
                s.wait_ge(p.ve.h, ve_m)
                s.activation(et2[:, :], ps_st2[:, :], AF.Exp)
                se_e = p.mark("scalar")
                t.wait_ge(p.se.h, se_e)
                for r in range(4):
                    vtile = ((bi - 1 + r // 2) * 2 + r % 2) % 4
                    t.matmul(ps_av2[:, :],
                             v_t[:, vtile * VW2 + h * 65:
                                 vtile * VW2 + h * 65 + 65],
                             et2[:, r * W:(r + 1) * W],
                             start=(r == 0), stop=(r == 3))
                pe_av = p.mark("tensor")
                v.wait_ge(p.pe.h, pe_av)
                v.reciprocal(rz2[:, :], ps_av2[64:65, :])
                ve_rz = p.mark("vector")
                t.wait_ge(p.ve.h, ve_rz)
                t.matmul(ps_rz2[:, :], cx.ones1_f[:, 0:64], rz2[:, :],
                         start=True, stop=True)
                pe_rz = p.mark("tensor")
                s.wait_ge(p.pe.h, pe_rz)
                s.wait_ge(p.ve.h, p.ve.n)
                s.activation(avt_[:, :], ps_av2[0:64, :], AF.Copy)
                se_av = p.mark("scalar")
                v.wait_ge(p.se.h, se_av)
                v.tensor_mul(av_all[base:base + 64, c * W:(c + 1) * W],
                             avt_[:, :], ps_rz2[:, :])
                p.mark("vector")

            for oc in range(NCH):
                p.wait_all("tensor", p.ve, p.se)
                for ic in range(NCH):
                    t.matmul(ps_o2[:, :],
                             wo_sb2[:, ic * D + oc * 128:ic * D + oc * 128 + 128],
                             av_all[:, ic * W:(ic + 1) * W],
                             start=(ic == 0), stop=(ic == NCH - 1))
                pe_o = p.mark("tensor")
                v.wait_ge(p.pe.h, pe_o)
                v.wait_ge(xls[slot].h, xls[slot].n)
                v.wait_ge(p.st.h, p.st.n)
                v.tensor_add(xres2[:, :], ps_o2[:, :],
                             buf[:, oc * 257 + 1:oc * 257 + 1 + W])
                ve_x = p.mark("vector")
                y.wait_ge(p.ve.h, ve_x)
                y.dma_start(out=xout_h[oc * 128:(oc + 1) * 128,
                                       b * W:(b + 1) * W],
                            in_=xres2[:, :])
                p.st.inc(p.last("sync"), 16)
        xin_consumed[slot] = p.pe.n
        xin_consumed_ve[slot] = p.ve.n
    y.wait_ge(p.st.h, p.st.n)


FPAD = 2816
NFC = FPAD // 128


def emit_l0f_ffn(p, cx, xp_h, w1_h, b1_h, w2_h, b2_h, lng_h, lnb_h,
                 xout_h):
    t, v, s, g, y = p.t, p.v, p.s, p.g, p.y
    ST = 1024

    xpb = p.sb([128, 8 * (ST + 1)], F32, "xpb")
    xsh = p.sb([128, 8 * 512], BF16, "xsh")
    xn2 = p.sb([128, 8 * ST], BF16, "xn2")
    tmp2 = p.sb([128, 512], F32, "tmp2")
    hid = p.sb([128, NFC * ST], BF16, "hid")
    w1b = [p.sb([128, 8 * 128], BF16, f"w1b{i}") for i in range(2)]
    w2b = [p.sb([128, NFC * 128], BF16, f"w2b{i}") for i in range(2)]
    b1_sb = p.sb([128, 44], F32, "b1_sb")
    b2_sb = p.sb([128, 8], F32, "b2_sb")
    lng_sb = p.sb([128, 8], F32, "lng_sb")
    lnb_sb = p.sb([128, 8], F32, "lnb_sb")
    xr3 = p.sb([128, 512], F32, "xr3")

    ps_h1 = p.ps([128, 512], F32, "ps_h1")
    ps_h2 = p.ps([128, 512], F32, "ps_h2")
    ps_w2 = p.ps([128, 512], F32, "ps_w2")

    wl = p.sem("wl3")
    w1l = [p.sem("w1l0"), p.sem("w1l1")]
    w2l = [p.sem("w2l0"), p.sem("w2l1")]
    xpl = [p.sem("xpl0"), p.sem("xpl1")]

    for dst, src in ((b1_sb, b1_h), (b2_sb, b2_h), (lng_sb, lng_h),
                     (lnb_sb, lnb_h)):
        y.dma_start(out=dst[:, :], in_=src)
        wl.inc(p.last("sync"), 16)
    wl_tot = wl.n

    xp_consumed = [0, 0]
    w1_consumed = [0, 0]
    w2_consumed = [0, 0]

    for sti in range(2):
        sslot = sti % 2
        bufx = xpb
        col0 = 4 + sti * ST
        if sti == 1:
            g.wait_ge(p.pe.h, xp_consumed[0])
            g.wait_ge(p.se.h, p.se.n)
            g.wait_ge(p.ve.h, p.ve.n)
        g.dma_start(
            out=bufx[:, :].rearrange("p (c q) -> p c q", c=NCH),
            in_=xp_h[:, col0 - 1:col0 + ST].rearrange("(c p) q -> p c q",
                                                      p=128))
        xpl[sslot].inc(p.last("gpsimd"), 16)

        for half in range(2):
            p.wait_all("scalar", xpl[sslot])
            p.wait_all("scalar", p.pe, p.ve)
            if sti == 0 and half == 0:
                s.wait_ge(wl.h, wl_tot)
            for c in range(NCH):
                off = 1 if c < 4 else 0
                s.activation(xsh[:, c * 512:c * 512 + 512],
                             bufx[:, c * (ST + 1) + off + half * 512:
                                  c * (ST + 1) + off + half * 512 + 512],
                             AF.Copy)
            p.mark("scalar")
            chunks = [xsh[:, c * 512:(c + 1) * 512] for c in range(NCH)]

            def ffn_ln_out(c, ch, mb, rb, _half=half):
                v.tensor_sub(tmp2[:, :], ch, mb)
                v.tensor_mul(tmp2[:, :], tmp2[:, :], rb)
                bb = bass.AP(lnb_sb, c, [[8, 128], [0, 512]])
                v.scalar_tensor_tensor(
                    xn2[:, c * ST + _half * 512:c * ST + _half * 512 + 512],
                    tmp2[:, :], lng_sb[:, c:c + 1], bb, ALU.mult, ALU.add)
                p.mark("vector")
            cx.ln_block(chunks, 512, ffn_ln_out)

        p.wait_all("tensor", p.se, p.ve)
        for ofc in range(NFC):
            for which, fo in ((0, ofc), (1, ofc + NFC)):
                wslot = which
                if ofc >= 1:
                    g.wait_ge(p.pe.h, w1_consumed[wslot])
                g.dma_start(
                    out=w1b[wslot][:, :].rearrange("p (c f) -> p c f", c=NCH),
                    in_=w1_h[:, fo * 128:(fo + 1) * 128].rearrange(
                        "(c p) f -> p c f", p=128))
                w1l[wslot].inc(p.last("gpsimd"), 16)
            p.wait_all("tensor", w1l[0], w1l[1])
            for half in range(2):
                for which in range(2):
                    psx = ps_h1 if which == 0 else ps_h2
                    t.wait_ge(p.ve.h, p.ve.n)
                    t.wait_ge(p.se.h, p.se.n)
                    for c in range(NCH):
                        t.matmul(psx[:, :],
                                 w1b[which][:, c * 128:(c + 1) * 128],
                                 xn2[:, c * ST + half * 512:
                                     c * ST + half * 512 + 512],
                                 start=(c == 0), stop=(c == NCH - 1))
                    pe_h = p.mark("tensor")
                s.wait_ge(p.pe.h, pe_h)
                s.activation(tmp2[:, :], ps_h2[:, :], AF.Gelu,
                             bias=b1_sb[:, ofc + NFC:ofc + NFC + 1])
                se_g = p.mark("scalar")
                v.wait_ge(p.se.h, se_g)
                v.scalar_tensor_tensor(
                    hid[:, ofc * ST + half * 512:ofc * ST + half * 512 + 512],
                    ps_h1[:, :], b1_sb[:, ofc:ofc + 1], tmp2[:, :],
                    ALU.add, ALU.mult)
                p.mark("vector")
            w1_consumed[0] = w1_consumed[1] = p.pe.n
        for oc in range(NCH):
            wslot = oc % 2
            if oc >= 2:
                g.wait_ge(p.pe.h, w2_consumed[wslot])
            g.dma_start(
                out=w2b[wslot][:, :].rearrange("p (f o) -> p f o", f=NFC),
                in_=w2_h[:, oc * 128:(oc + 1) * 128].rearrange(
                    "(f p) o -> p f o", p=128))
            w2l[wslot].inc(p.last("gpsimd"), 16)
            p.wait_all("tensor", w2l[wslot], p.ve, p.se)
            for half in range(2):
                t.wait_ge(p.ve.h, p.ve.n)
                for fc in range(NFC):
                    t.matmul(ps_w2[:, :],
                             w2b[wslot][:, fc * 128:(fc + 1) * 128],
                             hid[:, fc * ST + half * 512:
                                 fc * ST + half * 512 + 512],
                             start=(fc == 0), stop=(fc == NFC - 1))
                pe_w2 = p.mark("tensor")
                v.wait_ge(p.pe.h, pe_w2)
                v.wait_ge(p.st.h, p.st.n)
                v.scalar_tensor_tensor(
                    xr3[:, :], ps_w2[:, :], b2_sb[:, oc:oc + 1],
                    bufx[:, oc * (ST + 1) + 1 + half * 512:
                         oc * (ST + 1) + 1 + half * 512 + 512],
                    ALU.add, ALU.add)
                ve_x = p.mark("vector")
                y.wait_ge(p.ve.h, ve_x)
                y.dma_start(
                    out=xout_h[oc * 128:(oc + 1) * 128,
                               sti * ST + half * 512:
                               sti * ST + half * 512 + 512],
                    in_=xr3[:, :])
                p.st.inc(p.last("sync"), 16)
            w2_consumed[wslot] = p.pe.n
        xp_consumed[sslot] = p.pe.n
    y.wait_ge(p.st.h, p.st.n)


def build_l0f_attn():
    nc = bass.Bass("TRN2", target_bir_lowering=False)
    a = {}
    a["xin"] = nc.dram_tensor("xin", [D, HB + HL], F32, kind="ExternalInput").ap()
    a["wqkv"] = nc.dram_tensor("wqkv", [D, 3 * D], F32, kind="ExternalInput").ap()
    a["wo"] = nc.dram_tensor("wo", [D, D], F32, kind="ExternalInput").ap()
    a["qs"] = nc.dram_tensor("qs", [128, 1], F32, kind="ExternalInput").ap()
    a["ks"] = nc.dram_tensor("ks", [128, 1], F32, kind="ExternalInput").ap()
    for nm in ("tq1", "tq2", "tk1", "tk2"):
        a[nm] = nc.dram_tensor(nm, [128, HB + HL], F32, kind="ExternalInput").ap()
    a["rt"] = nc.dram_tensor("rt", [128, 128], F32, kind="ExternalInput").ap()
    a["hm"] = nc.dram_tensor("hm", [8, 128, 16], F32, kind="ExternalInput").ap()
    a["hmt"] = nc.dram_tensor("hmt", [8, 16, 128], F32, kind="ExternalInput").ap()
    a["oh"] = nc.dram_tensor("oh", [16, 16 * 128], F32, kind="ExternalInput").ap()
    a["amask"] = nc.dram_tensor("amask", [2, 4, 128, W], F32,
                                kind="ExternalInput").ap()
    xout = nc.dram_tensor("xout", [D, HL], F32, kind="ExternalOutput").ap()
    with ExitStack() as stack:
        p = Prog(nc, stack)
        cx = Ctx(p)
        emit_l0f_attn(p, cx, a["xin"], a["wqkv"], a["wo"], a["qs"], a["ks"],
                      a["tq1"], a["tq2"], a["tk1"], a["tk2"], a["rt"],
                      a["hm"], a["hmt"], a["oh"], a["amask"], xout)
        block = stack.enter_context(nc.Block())
        p.replay(block)
    return nc


def build_l0f_ffn():
    nc = bass.Bass("TRN2", target_bir_lowering=False)
    xp = nc.dram_tensor("xp", [D, 4 + HL], F32, kind="ExternalInput").ap()
    w1 = nc.dram_tensor("w1", [D, 2 * FPAD], F32, kind="ExternalInput").ap()
    b1 = nc.dram_tensor("b1", [128, 44], F32, kind="ExternalInput").ap()
    w2 = nc.dram_tensor("w2", [FPAD, D], F32, kind="ExternalInput").ap()
    b2 = nc.dram_tensor("b2", [128, 8], F32, kind="ExternalInput").ap()
    lng = nc.dram_tensor("lng", [128, 8], F32, kind="ExternalInput").ap()
    lnb = nc.dram_tensor("lnb", [128, 8], F32, kind="ExternalInput").ap()
    xout = nc.dram_tensor("xout", [D, HL], F32, kind="ExternalOutput").ap()
    with ExitStack() as stack:
        p = Prog(nc, stack)
        cx = Ctx(p)
        emit_l0f_ffn(p, cx, xp, w1, b1, w2, b2, lng, lnb, xout)
        block = stack.enter_context(nc.Block())
        p.replay(block)
    return nc



def rope_tables(j):
    t0 = j * HL
    pos = np.arange(t0 - HB, t0 + HL, dtype=np.float64)
    d = np.arange(128) % 64
    half = (d % 32) * 2.0
    inv_freq = 1.0 / (10000.0 ** (half / DH))
    freqs = pos[None, :] * inv_freq[:, None]
    base_scale = (half + 0.4 * DH) / (1.4 * DH)
    power = (pos - L // 2) / (W // 2)
    scale = base_scale[:, None] ** power[None, :]
    cos, sin = np.cos(freqs), np.sin(freqs)
    QK = 8.0
    tq1 = (QK * cos * scale).astype(np.float32)
    tq2 = (QK * sin * scale).astype(np.float32)
    tk1 = (cos / scale).astype(np.float32)
    tk2 = (sin / scale).astype(np.float32)
    return tq1, tq2, tk1, tk2


def l0f_consts(j):
    rt = np.zeros((128, 128), dtype=np.float32)
    for p_ in range(128):
        dd = p_ % 64
        hb = p_ - dd
        if dd < 32:
            rt[p_, hb + dd + 32] = 1.0
        else:
            rt[p_, hb + dd - 32] = -1.0
    rt = np.zeros((128, 128), dtype=np.float32)
    for m in range(128):
        dd = m % 64
        hb = m - dd
        if dd < 32:
            rt[hb + dd + 32, m] = -1.0
        else:
            rt[hb + dd - 32, m] = 1.0
    hm = np.zeros((8, 128, 16), dtype=np.float32)
    for c in range(8):
        hm[c, 0:64, 2 * c] = 1.0
        hm[c, 64:128, 2 * c + 1] = 1.0
    hmt = np.transpose(hm, (0, 2, 1)).copy()
    oh = np.zeros((16, 16 * 128), dtype=np.float32)
    for h in range(16):
        oh[h, h * 128:(h + 1) * 128] = 1.0
    kk = np.arange(128)
    qq = np.arange(W)
    amask = np.zeros((2, 4, 128, W), dtype=np.float32)
    for r in range(4):
        if r < 2:
            m = (r * 128 + kk[:, None]) < qq[None, :]
        else:
            m = ((r - 2) * 128 + kk[:, None]) > qq[None, :]
        amask[1, r][m] = NEGBIG
    amask[0] = amask[1]
    if j == 0:
        amask[0, 0] = NEGBIG
        amask[0, 1] = NEGBIG
    return rt, hm, hmt, oh, amask


def stage_l0f_attn(inputs, layer, xin_full):
    maps = []
    Wqkv = np.asarray(inputs["L0_ffn_Wqkv"][layer])
    Wo = np.asarray(inputs["L0_ffn_Wo"][layer])
    qsv = np.asarray(inputs["L0_ffn_q_scale"][layer])
    ksv = np.asarray(inputs["L0_ffn_k_scale"][layer])
    qs = np.tile(qsv, 2).reshape(128, 1).astype(np.float32)
    ks = np.tile(ksv, 2).reshape(128, 1).astype(np.float32)
    for c in range(NCORES):
        b, j = divmod(c, 2)
        t0 = j * HL
        xin = np.zeros((D, HB + HL), dtype=np.float32)
        lo = max(0, t0 - HB)
        xin[:, HB - (t0 - lo):] = xin_full[b, lo:t0 + HL].T
        tq1, tq2, tk1, tk2 = rope_tables(j)
        rt, hm, hmt, oh, amask = l0f_consts(j)
        maps.append(dict(xin=np.ascontiguousarray(xin), wqkv=Wqkv, wo=Wo,
                         qs=qs, ks=ks, tq1=tq1, tq2=tq2, tk1=tk1, tk2=tk2,
                         rt=rt, hm=hm, hmt=hmt, oh=oh, amask=amask))
    return maps


def stage_l0f_ffn(inputs, layer, xp_full):
    maps = []
    W1 = np.asarray(inputs["L0_ffn_W1"][layer])
    b1 = np.asarray(inputs["L0_ffn_b1"][layer])
    W2 = np.asarray(inputs["L0_ffn_W2"][layer])
    b2 = np.asarray(inputs["L0_ffn_b2"][layer])
    lng = np.asarray(inputs["L0_ffn_ln_g"][layer])
    lnb = np.asarray(inputs["L0_ffn_ln_b"][layer])
    DI = 2730
    w1p = np.zeros((D, 2 * FPAD), dtype=np.float32)
    w1p[:, 0:DI] = W1[:, 0:DI]
    w1p[:, FPAD:FPAD + DI] = W1[:, DI:2 * DI]
    b1p = np.zeros((2 * FPAD,), dtype=np.float32)
    b1p[0:DI] = b1[0:DI]
    b1p[FPAD:FPAD + DI] = b1[DI:]
    b1m = np.ascontiguousarray(b1p.reshape(44, 128).T).astype(np.float32)
    w2p = np.zeros((FPAD, D), dtype=np.float32)
    w2p[0:DI] = W2
    b2m = np.ascontiguousarray(b2.reshape(8, 128).T).astype(np.float32)
    lngm = np.ascontiguousarray(lng.reshape(8, 128).T).astype(np.float32)
    lnbm = np.ascontiguousarray(lnb.reshape(8, 128).T).astype(np.float32)
    for c in range(NCORES):
        b, j = divmod(c, 2)
        t0 = j * HL
        xp = np.zeros((D, 4 + HL), dtype=np.float32)
        lo = max(0, t0 - 4)
        xp[:, 4 - (t0 - lo):] = xp_full[b, lo:t0 + HL].T
        maps.append(dict(xp=np.ascontiguousarray(xp), w1=w1p, b1=b1m,
                         w2=w2p, b2=b2m, lng=lngm, lnb=lnbm))
    return maps


HT = 2048


def emit_l1_attn(p, cx, xg_h, winx_h, winz_h, convw_h, convb_h, ldt_h,
                 lar_h, aim_h, cre_h, cim_h, dvec_h, wout_h, pout_h,
                 xg_tile=None, pout_ap=None):
    t, v, s, g, y = p.t, p.v, p.s, p.g, p.y
    if xg_tile is None:
        xg_tile = lambda tt: xg_h[:, tt * 512:(tt + 1) * 512]
    if pout_ap is None:
        pout_ap = lambda dc, tt: pout_h[dc * 128:(dc + 1) * 128,
                                        tt * 512:(tt + 1) * 512]
    xz_hbm = p.nc.dram_tensor("xz_hbm", [NCH, 128, L], BF16).ap()
    yg_hbm = p.nc.dram_tensor("yg_hbm", [NCH, 128, L], BF16).ap()

    xz_t = p.sb([128, L], BF16, "xz_t")
    xgt = p.sb([128, 8 * 512], BF16, "xgt")
    ygt = xgt
    u_b = p.sb([128, L], BF16, "u_b")
    phi = p.sb([128, HT], F32, "phi")
    ph2 = p.sb([128, HT], F32, "ph2")
    kin = p.sb([128, HT], I32, "kin")
    ctb_ = [p.sb([128, HT], BF16, f"ctb{i}") for i in range(2)]
    stb_ = [p.sb([128, HT], BF16, f"stb{i}") for i in range(2)]
    p1b_ = [p.sb([128, HT], BF16, f"p1b{i}") for i in range(2)]
    p2b_ = [p.sb([128, HT], BF16, f"p2b{i}") for i in range(2)]
    mrb_ = [p.sb([128, HT], BF16, f"mrb{i}") for i in range(2)]
    mib_ = [p.sb([128, HT], BF16, f"mib{i}") for i in range(2)]
    srb = p.sb([128, L], BF16, "srb")
    sib = p.sb([128, L], BF16, "sib")
    yac = p.sb([128, L], F32, "yac")
    acc = yac[:, :]
    d1b_ = [p.sb([128, HT], BF16, f"d1b{i}") for i in range(2)]
    d2b_ = [p.sb([128, HT], BF16, f"d2b{i}") for i in range(2)]
    tvec = p.sb([128, HT], F32, "tvec")
    zsc = p.sb([128, 512], F32, "zsc")
    zb = p.sb([128, 512], BF16, "zbuf")
    winx_sb = p.sb([128, 8 * D], BF16, "winx_sb")
    winz_sb = p.sb([128, 8 * D], BF16, "winz_sb")
    wout_sb = winx_sb
    cw_sb = p.sb([128, 8 * 4], F32, "cw_sb")
    cb_sb = p.sb([128, 8], F32, "cb_sb")
    ldt_sb = p.sb([128, 8], F32, "ldt_sb")
    lar_sb = p.sb([128, 64], F32, "lar_sb")
    aim_sb = p.sb([128, 64], F32, "aim_sb")
    cre_sb = p.sb([128, 64], F32, "cre_sb")
    cim_sb = p.sb([128, 64], F32, "cim_sb")
    dv_sb = p.sb([128, 8], F32, "dv_sb")
    dtc = p.sb([128, 1], F32, "dtc")
    arn = p.sb([128, 8], F32, "arn")
    rho = p.sb([128, 8], F32, "rho")
    th2 = p.sb([128, 8], F32, "th2")
    thn = p.sb([128, 8], F32, "thn")
    ere = p.sb([128, 8], F32, "ere")
    eim = p.sb([128, 8], F32, "eim")
    ctr = p.sb([128, 8], F32, "ctr")
    cti = p.sb([128, 8], F32, "cti")
    sc1 = p.sb([128, 8], F32, "sc1")
    sc2 = p.sb([128, 8], F32, "sc2")
    sc3 = p.sb([128, 8], F32, "sc3")
    hpi = p.sb([128, 1], F32, "hpi")
    aab = p.sb([128, 8], F32, "aab")
    sgb = p.sb([128, 8], F32, "sgb")
    cimf = p.sb([128, 8], F32, "cimf")
    out_f = zsc

    ps_x = p.ps([128, 512], F32, "ps_x")
    ps_z = p.ps([128, 512], F32, "ps_z")
    ps_w = p.ps([128, 512], F32, "ps_w")

    wl = p.sem("wl4")
    xl = [p.sem("xl40"), p.sem("xl41")]

    v.memset(hpi[:, :], float(np.pi / 2))
    p.mark("vector")
    g.iota(tvec[:, :], [[1, HT]], channel_multiplier=0,
           allow_small_or_imprecise_dtypes=True)

    for c in range(NCH):
        g.dma_start(out=winx_sb[:, c * D:(c + 1) * D],
                    in_=winx_h[c * 128:(c + 1) * 128, :])
        wl.inc(p.last("gpsimd"), 16)
        g.dma_start(out=winz_sb[:, c * D:(c + 1) * D],
                    in_=winz_h[c * 128:(c + 1) * 128, :])
        wl.inc(p.last("gpsimd"), 16)
    for dst, src in ((cw_sb, convw_h), (cb_sb, convb_h), (ldt_sb, ldt_h),
                     (lar_sb, lar_h), (aim_sb, aim_h), (cre_sb, cre_h),
                     (cim_sb, cim_h), (dv_sb, dvec_h)):
        y.dma_start(out=dst[:, :], in_=src)
        p.ld.inc(p.last("sync"), 16)
    wl_tot, ld_tot = wl.n, p.ld.n

    for tt in range(8):
        sl = tt % 2
        g.wait_ge(p.pe.h, p.pe.n)
        g.dma_start(out=xgt[:, :].rearrange("p (c q) -> p c q", c=NCH),
                    in_=xg_tile(tt).rearrange("(c p) q -> p c q", p=128))
        xl[sl].inc(p.last("gpsimd"), 16)
        p.wait_all("tensor", xl[sl])
        if tt == 0:
            t.wait_ge(wl.h, wl_tot)
        for cc in range(NCH):
            p.wait_all("tensor", p.se)
            for c in range(NCH):
                t.matmul(ps_x[:, :], winx_sb[:, c * D + cc * 128:
                                             c * D + cc * 128 + 128],
                         xgt[:, c * 512:(c + 1) * 512],
                         start=(c == 0), stop=(c == NCH - 1))
            pe_x = p.mark("tensor")
            s.wait_ge(p.pe.h, pe_x)
            s.wait_ge(p.st.h, p.st.n)
            s.activation(xz_t[:, (cc % 2) * 512:(cc % 2) * 512 + 512],
                         ps_x[:, :], AF.Copy)
            se_x = p.mark("scalar")
            y.wait_ge(p.se.h, se_x)
            y.dma_start(out=xz_hbm[cc, :, tt * 512:(tt + 1) * 512],
                        in_=xz_t[:, (cc % 2) * 512:(cc % 2) * 512 + 512])
            p.st.inc(p.last("sync"), 16)

    xzl = p.sem("xzl")
    it_prev = {"se_tab": 0, "gp_d": [0, 0], "ve_m": [0, 0]}
    for cc in range(NCH):
        xz_c = xz_t[:, :]
        y.wait_ge(p.st.h, p.st.n)
        y.wait_ge(p.ve.h, p.ve.n)
        y.dma_start(out=xz_t[:, :], in_=xz_hbm[cc, :, :])
        xzl.inc(p.last("sync"), 16)
        p.wait_all("vector", p.se, p.gd, xzl)
        v.wait_ge(p.ld.h, ld_tot)
        v.tensor_scalar(acc[:, :], xz_c, cw_sb[:, cc * 4 + 3:cc * 4 + 4],
                        None, ALU.mult)
        for kk in range(3):
            w_col = cw_sb[:, cc * 4 + 2 - kk:cc * 4 + 3 - kk]
            v.scalar_tensor_tensor(acc[:, 1 + kk:L], xz_c[:, 0:L - 1 - kk],
                                   w_col, acc[:, 1 + kk:L],
                                   ALU.mult, ALU.add)
        ve_cv = p.mark("vector")
        s.wait_ge(p.ve.h, ve_cv)
        s.activation(u_b[:, :], acc[:, :], AF.Silu,
                     bias=cb_sb[:, cc:cc + 1])
        se_u = p.mark("scalar")

        p.wait_all("vector", p.se)
        v.tensor_copy(dtc[:, :], ldt_sb[:, cc:cc + 1])
        ve_pre = p.mark("vector")
        s.wait_ge(p.ve.h, ve_pre)
        s.activation(dtc[:, :], dtc[:, :], AF.Exp)
        s.activation(arn[:, :], lar_sb[:, cc * 8:(cc + 1) * 8], AF.Exp)
        se_pre = p.mark("scalar")
        v.wait_ge(p.se.h, se_pre)
        aimc_r = aim_sb[:, cc * 8:(cc + 1) * 8]
        s.activation(aab[:, :], aimc_r, AF.Abs)
        s.activation(sgb[:, :], aimc_r, AF.Sign)
        se_abs = p.mark("scalar")
        v.wait_ge(p.se.h, se_abs)
        v.tensor_mul(cimf[:, :], cim_sb[:, cc * 8:(cc + 1) * 8], sgb[:, :])
        v.tensor_scalar(thn[:, :], aab[:, :],
                        dtc[:, 0:1], None, ALU.mult)
        v.tensor_single_scalar(th2[:, :], thn[:, :], float(1 / (2 * np.pi)),
                               ALU.mult)
        v.tensor_scalar(sc1[:, :], arn[:, :], dtc[:, 0:1], None, ALU.mult)
        v.tensor_single_scalar(sc1[:, :], sc1[:, :], -1.0, ALU.mult)
        ve_s1 = p.mark("vector")
        s.wait_ge(p.ve.h, ve_s1)
        s.activation(rho[:, :], sc1[:, :], AF.Exp)
        s.activation(ere[:, :], thn[:, :], AF.Sin, bias=hpi[:, 0:1],
                     scale=-1.0)
        s.activation(eim[:, :], thn[:, :], AF.Sin)
        se_s2 = p.mark("scalar")
        v.wait_ge(p.se.h, se_s2)
        v.tensor_mul(ere[:, :], ere[:, :], rho[:, :])
        v.tensor_mul(eim[:, :], eim[:, :], rho[:, :])
        v.tensor_single_scalar(ere[:, :], ere[:, :], 1.0, ALU.subtract)
        v.tensor_mul(sc2[:, :], arn[:, :], arn[:, :])
        aimc = aab[:, :]
        v.scalar_tensor_tensor(sc3[:, :], aimc, 1.0, aimc, ALU.mult,
                               ALU.mult)
        v.tensor_add(sc2[:, :], sc2[:, :], sc3[:, :])
        v.reciprocal(sc2[:, :], sc2[:, :])
        v.tensor_mul(sc1[:, :], ere[:, :], arn[:, :])
        v.tensor_mul(sc3[:, :], eim[:, :], aimc)
        v.tensor_sub(ctr[:, :], sc3[:, :], sc1[:, :])
        v.tensor_mul(sc1[:, :], eim[:, :], arn[:, :])
        v.tensor_mul(sc3[:, :], ere[:, :], aimc)
        v.tensor_add(sc1[:, :], sc1[:, :], sc3[:, :])
        v.tensor_single_scalar(sc1[:, :], sc1[:, :], -1.0, ALU.mult)
        crec = cre_sb[:, cc * 8:(cc + 1) * 8]
        cimc = cimf[:, :]
        v.tensor_mul(sc3[:, :], crec, ctr[:, :])
        v.tensor_mul(eim[:, :], cimc, sc1[:, :])
        v.tensor_sub(sc3[:, :], sc3[:, :], eim[:, :])
        v.tensor_mul(sc3[:, :], sc3[:, :], sc2[:, :])
        v.tensor_single_scalar(sc3[:, :], sc3[:, :], 2.0, ALU.mult)
        v.tensor_mul(ere[:, :], crec, sc1[:, :])
        v.tensor_mul(eim[:, :], cimc, ctr[:, :])
        v.tensor_add(sc1[:, :], ere[:, :], eim[:, :])
        v.tensor_mul(sc1[:, :], sc1[:, :], sc2[:, :])
        v.tensor_single_scalar(sc1[:, :], sc1[:, :], 2.0, ALU.mult)
        ve_ct = p.mark("vector")

        v.memset(yac[:, :], 0.0)
        p.mark("vector")
        it_state = {"se_tab": it_prev["se_tab"], "gp_d": it_prev["gp_d"],
                    "ve_m": it_prev["ve_m"]}

        for n in range(8):
            rho_c = rho[:, n:n + 1]
            th2_c = th2[:, n:n + 1]
            ctr_c = sc3[:, n:n + 1]
            cti_c = sc1[:, n:n + 1]
            for hh in range(2):
                pb = (n * 2 + hh) % 2
                ctb, stb = ctb_[pb], stb_[pb]
                p1b, p2b = p1b_[pb], p2b_[pb]
                mrb, mib = mrb_[pb], mib_[pb]
                d1b, d2b = d1b_[pb], d2b_[pb]
                t0c = hh * HT
                if it_state["se_tab"]:
                    v.wait_ge(p.se.h, it_state["se_tab"])
                if n == 0 and hh == 0:
                    p.wait_all("vector", p.se, p.gd)
                v.tensor_scalar(phi[:, :], tvec[:, :], th2_c, None,
                                ALU.mult)
                if hh == 1:
                    v.tensor_single_scalar(sc2[:, 0:1], th2_c, float(HT),
                                           ALU.mult)
                    v.tensor_scalar(phi[:, :], phi[:, :], sc2[:, 0:1], None,
                                    ALU.add)
                ve_ph0 = p.mark("vector")
                g.wait_ge(p.ve.h, ve_ph0)
                g.tensor_copy(kin[:, :], phi[:, :])
                g.tensor_sub(phi[:, :], phi[:, :], kin[:, :])
                gp_r = p.gd.inc(p.last("gpsimd"))
                v.wait_ge(p.gd.h, gp_r)
                v.tensor_single_scalar(ph2[:, :], phi[:, :], 0.25, ALU.add)
                v.tensor_single_scalar(d2b[:, :], ph2[:, :], 0.5, ALU.is_ge)
                ve_ph = p.mark("vector")
                g.wait_ge(p.ve.h, ve_ph)
                g.tensor_sub(ph2[:, :], ph2[:, :], d2b[:, :])
                gp_ph = p.gd.inc(p.last("gpsimd"))
                s.wait_ge(p.gd.h, max(gp_ph, it_state["gp_d"][pb]))
                s.activation(stb[:, :], phi[:, :], AF.Sin,
                             scale=float(2 * np.pi))
                s.activation(ctb[:, :], ph2[:, :], AF.Sin,
                             scale=float(2 * np.pi))
                se_tab = p.mark("scalar")
                it_state["se_tab"] = se_tab
                g.wait_ge(p.se.h, max(se_tab, se_u))
                if it_state["ve_m"][pb]:
                    g.wait_ge(p.ve.h, it_state["ve_m"][pb])
                usl = u_b[:, t0c:t0c + HT]
                g.tensor_mul(p1b[:, :], usl, ctb[:, :])
                g.tensor_mul(p2b[:, :], usl, stb[:, :])
                gp_m = p.gd.inc(p.last("gpsimd"))
                v.wait_ge(p.gd.h, gp_m)
                v.tensor_scalar(mrb[:, :], p2b[:, :], cti_c, None, ALU.mult)
                v.scalar_tensor_tensor(mrb[:, :], p1b[:, :], ctr_c,
                                       mrb[:, :], ALU.mult, ALU.add)
                v.tensor_scalar(mib[:, :], p2b[:, :], ctr_c, None, ALU.mult)
                v.tensor_single_scalar(mib[:, :], mib[:, :], -1.0, ALU.mult)
                v.scalar_tensor_tensor(mib[:, :], p1b[:, :], cti_c,
                                       mib[:, :], ALU.mult, ALU.add)
                rho_b = bass.AP(rho, n, [[8, 128], [0, HT]])
                init_r = 0.0 if hh == 0 else srb[:, HT - 1:HT]
                init_i = 0.0 if hh == 0 else sib[:, HT - 1:HT]
                v.tensor_tensor_scan(srb[:, t0c:t0c + HT], rho_b, mrb[:, :],
                                     init_r, ALU.mult, ALU.add)
                v.tensor_tensor_scan(sib[:, t0c:t0c + HT], rho_b, mib[:, :],
                                     init_i, ALU.mult, ALU.add)
                ve_sc = p.mark("vector")
                it_state["ve_m"][pb] = ve_sc
                g.wait_ge(p.ve.h, ve_sc)
                g.tensor_mul(d1b[:, :], ctb[:, :], srb[:, t0c:t0c + HT])
                g.tensor_mul(d2b[:, :], stb[:, :], sib[:, t0c:t0c + HT])
                g.tensor_sub(d1b[:, :], d1b[:, :], d2b[:, :])
                gp_d = p.gd.inc(p.last("gpsimd"))
                it_state["gp_d"][pb] = gp_d
                v.wait_ge(p.gd.h, gp_d)
                v.tensor_add(yac[:, t0c:t0c + HT], yac[:, t0c:t0c + HT],
                             d1b[:, :])
                p.mark("vector")
        it_prev = it_state
        p.wait_all("vector", p.se)
        v.scalar_tensor_tensor(yac[:, :], u_b[:, :], dv_sb[:, cc:cc + 1],
                               yac[:, :], ALU.mult, ALU.add)
        ve_y = p.mark("vector")

        for tt in range(8):
            sl = tt % 2
            g.wait_ge(p.pe.h, p.pe.n)
            g.dma_start(out=xgt[:, :].rearrange("p (c q) -> p c q", c=NCH),
                        in_=xg_tile(tt).rearrange("(c p) q -> p c q", p=128))
            xl[sl].inc(p.last("gpsimd"), 16)
            p.wait_all("tensor", xl[sl])
            p.wait_all("tensor", p.se, p.ve)
            for c in range(NCH):
                t.matmul(ps_z[:, :], winz_sb[:, c * D + cc * 128:
                                             c * D + cc * 128 + 128],
                         xgt[:, c * 512:(c + 1) * 512],
                         start=(c == 0), stop=(c == NCH - 1))
            pe_z = p.mark("tensor")
            s.wait_ge(p.pe.h, pe_z)
            s.activation(zb[:, :], ps_z[:, :], AF.Silu)
            se_z = p.mark("scalar")
            v.wait_ge(p.se.h, se_z)
            v.wait_ge(p.st.h, p.st.n)
            v.tensor_mul(xz_c[:, tt * 512:(tt + 1) * 512],
                         yac[:, tt * 512:(tt + 1) * 512], zb[:, :])
            ve_g = p.mark("vector")
            y.wait_ge(p.ve.h, ve_g)
            y.dma_start(out=yg_hbm[cc, :, tt * 512:(tt + 1) * 512],
                        in_=xz_c[:, tt * 512:(tt + 1) * 512])
            p.st.inc(p.last("sync"), 16)

    g.wait_ge(p.pe.h, p.pe.n)
    for c in range(NCH):
        g.dma_start(out=wout_sb[:, c * D:(c + 1) * D],
                    in_=wout_h[c * 128:(c + 1) * 128, :])
        wl.inc(p.last("gpsimd"), 16)
    wl_tot2 = wl.n
    p.wait_all("tensor", wl)
    ygl = p.sem("ygl")
    for tt in range(8):
        y.wait_ge(p.st.h, p.st.n)
        y.wait_ge(p.pe.h, p.pe.n)
        y.dma_start(out=ygt[:, :].rearrange("p (c q) -> p c q", c=NCH),
                    in_=yg_hbm[:, :, tt * 512:(tt + 1) * 512].rearrange(
                        "c p q -> p c q"))
        ygl.inc(p.last("sync"), 16)
        p.wait_all("tensor", ygl)
        for dc in range(NCH):
            p.wait_all("tensor", p.se, p.ve)
            for cc in range(NCH):
                t.matmul(ps_w[:, :], wout_sb[:, cc * D + dc * 128:
                                             cc * D + dc * 128 + 128],
                         ygt[:, cc * 512:(cc + 1) * 512],
                         start=(cc == 0), stop=(cc == NCH - 1))
            pe_w = p.mark("tensor")
            v.wait_ge(p.pe.h, pe_w)
            v.wait_ge(p.st.h, p.st.n)
            v.tensor_copy(out_f[:, :], ps_w[:, :])
            ve_o = p.mark("vector")
            g.wait_ge(p.ve.h, ve_o)
            g.dma_start(out=pout_ap(dc, tt), in_=out_f[:, :])
            p.st.inc(p.last("gpsimd"), 16)
    y.wait_ge(p.st.h, p.st.n)


def build_l1_attn():
    nc = bass.Bass("TRN2", target_bir_lowering=False)
    xg = nc.dram_tensor("xg", [D, L], F32, kind="ExternalInput").ap()
    winx = nc.dram_tensor("winx", [D, D], F32, kind="ExternalInput").ap()
    winz = nc.dram_tensor("winz", [D, D], F32, kind="ExternalInput").ap()
    convw = nc.dram_tensor("convw", [128, 32], F32, kind="ExternalInput").ap()
    convb = nc.dram_tensor("convb", [128, 8], F32, kind="ExternalInput").ap()
    ldt = nc.dram_tensor("ldt", [128, 8], F32, kind="ExternalInput").ap()
    lar = nc.dram_tensor("lar", [128, 64], F32, kind="ExternalInput").ap()
    aim = nc.dram_tensor("aim", [128, 64], F32, kind="ExternalInput").ap()
    cre = nc.dram_tensor("cre", [128, 64], F32, kind="ExternalInput").ap()
    cim = nc.dram_tensor("cim", [128, 64], F32, kind="ExternalInput").ap()
    dvec = nc.dram_tensor("dvec", [128, 8], F32, kind="ExternalInput").ap()
    wout = nc.dram_tensor("wout", [D, D], F32, kind="ExternalInput").ap()
    pout = nc.dram_tensor("pout", [D, L], F32, kind="ExternalOutput").ap()
    with ExitStack() as stack:
        p = Prog(nc, stack)
        cx = Ctx(p)
        emit_l1_attn(p, cx, xg, winx, winz, convw, convb, ldt, lar, aim, cre,
                     cim, dvec, wout, pout)
        block = stack.enter_context(nc.Block())
        p.replay(block)
    return nc


def _cm(x):
    return np.ascontiguousarray(np.asarray(x).reshape(8, 128).T).astype(
        np.float32)


def _cm2(x):
    a = np.asarray(x)
    k = a.shape[1]
    out = np.zeros((128, 8 * k), dtype=np.float32)
    for cc in range(8):
        out[:, cc * k:(cc + 1) * k] = a[cc * 128:(cc + 1) * 128]
    return out


def stage_l1_attn(inputs, xg_full):
    maps = []
    W_in = np.asarray(inputs["L1_attn_W_in"])
    W_out = np.asarray(inputs["L1_attn_W_out"])
    cw = np.asarray(inputs["L1_attn_conv_w"])
    cb = np.asarray(inputs["L1_attn_conv_b"])
    ldt = np.asarray(inputs["L1_attn_log_dt"])
    lar = np.asarray(inputs["L1_attn_log_A_real"])
    aim = np.asarray(inputs["L1_attn_A_imag"])
    cre = np.asarray(inputs["L1_attn_C_re"])
    cim = np.asarray(inputs["L1_attn_C_im"])
    dvec = np.asarray(inputs["L1_attn_D"])
    H = 2048
    for c in range(NCORES):
        b, j = divmod(c, 2)
        ch0 = j * 1024
        rows = slice(ch0, ch0 + 1024)
        zrows = slice(H + ch0, H + ch0 + 1024)
        maps.append(dict(
            xg=np.ascontiguousarray(xg_full[b].T),
            winx=np.ascontiguousarray(W_in[rows].T),
            winz=np.ascontiguousarray(W_in[zrows].T),
            convw=_cm2(cw[rows]),
            convb=_cm(cb[rows]),
            ldt=_cm(ldt[rows]),
            lar=_cm2(lar[rows]),
            aim=_cm2(aim[rows]),
            cre=_cm2(cre[rows]),
            cim=_cm2(cim[rows]),
            dvec=_cm(dvec[rows]),
            wout=np.ascontiguousarray(W_out[:, rows].T),
        ))
    return maps



def emit_sandwich_add(p, cx, base_h, inner_h, out_res_h, out_ln_h,
                      out_ln_bf_h=None):
    t, v, s, g, y = p.t, p.v, p.s, p.g, p.y
    ib = p.sb([128, 8 * 512], BF16, "ep_ib")
    bs = p.sb([128, 8 * 512], F32, "ep_bs")
    rs = p.sb([128, 8 * 512], F32, "ep_rs")
    rb = p.sb([128, 8 * 512], BF16, "ep_rb")
    ol = p.sb([128, 8 * 512], F32, "ep_ol")
    tp = p.sb([128, 512], F32, "ep_tp")
    lds = p.sem("eld")
    for tt in range(4):
        g.wait_ge(p.ve.h, p.ve.n)
        g.dma_start(out=ib[:, :].rearrange("p (c q) -> p c q", c=NCH),
                    in_=inner_h[:, tt * 512:(tt + 1) * 512].rearrange(
                        "(c p) q -> p c q", p=128))
        lds.inc(p.last("gpsimd"), 16)
        y.wait_ge(p.ve.h, p.ve.n)
        y.dma_start(out=bs[:, :].rearrange("p (c q) -> p c q", c=NCH),
                    in_=base_h[:, tt * 512:(tt + 1) * 512].rearrange(
                        "(c p) q -> p c q", p=128))
        lds.inc(p.last("sync"), 16)
        p.wait_all("scalar", lds)
        p.wait_all("tensor", lds)
        p.wait_all("vector", lds)
        chunks = [ib[:, c * 512:(c + 1) * 512] for c in range(NCH)]

        def add_out(c, ch, mb, rb_):
            v.tensor_sub(tp[:, :], ch, mb)
            v.tensor_mul(tp[:, :], tp[:, :], rb_)
            v.tensor_add(rs[:, c * 512:(c + 1) * 512], tp[:, :],
                         bs[:, c * 512:(c + 1) * 512])
            v.tensor_copy(rb[:, c * 512:(c + 1) * 512],
                          rs[:, c * 512:(c + 1) * 512])
        cx.ln_block(chunks, 512, add_out)
        ve_r = p.ve.n
        y.wait_ge(p.ve.h, ve_r)
        y.dma_start(out=out_res_h[:, tt * 512:(tt + 1) * 512].rearrange(
                        "(c p) q -> p c q", p=128),
                    in_=rs[:, :].rearrange("p (c q) -> p c q", c=NCH))
        p.st.inc(p.last("sync"), 16)
        rbc = [rb[:, c * 512:(c + 1) * 512] for c in range(NCH)]

        def ln_out(c, ch, mb, rb_):
            v.tensor_sub(ol[:, c * 512:(c + 1) * 512],
                         rs[:, c * 512:(c + 1) * 512], mb)
            v.tensor_mul(ol[:, c * 512:(c + 1) * 512],
                         ol[:, c * 512:(c + 1) * 512], rb_)
        cx.ln_block(rbc, 512, ln_out)
        ve_l = p.ve.n
        y.wait_ge(p.ve.h, ve_l)
        y.dma_start(out=out_ln_h[:, tt * 512:(tt + 1) * 512].rearrange(
                        "(c p) q -> p c q", p=128),
                    in_=ol[:, :].rearrange("p (c q) -> p c q", c=NCH))
        p.st.inc(p.last("sync"), 16)
        if out_ln_bf_h is not None:
            g.wait_ge(p.ve.h, ve_l)
            g.dma_start(out=out_ln_bf_h[:, tt * 512:(tt + 1) * 512].rearrange(
                            "(c p) q -> p c q", p=128),
                        in_=ol[:, :].rearrange("p (c q) -> p c q", c=NCH))
            p.gd.inc(p.last("gpsimd"), 16)
    y.wait_ge(p.st.h, p.st.n)
    y.wait_ge(p.gd.h, p.gd.n)


def build_l0epi():
    nc = bass.Bass("TRN2", target_bir_lowering=False)
    xres1 = nc.dram_tensor("xres1", [D, HL], F32, kind="ExternalInput").ap()
    xin6 = nc.dram_tensor("xin6", [D, HL], F32, kind="ExternalInput").ap()
    xres2 = nc.dram_tensor("xres2", [D, HL], F32, kind="ExternalOutput").ap()
    xln2 = nc.dram_tensor("xln2", [D, HL], F32, kind="ExternalOutput").ap()
    with ExitStack() as stack:
        p = Prog(nc, stack)
        cx = Ctx(p)
        emit_sandwich_add(p, cx, xres1, xin6, xres2, xln2)
        block = stack.enter_context(nc.Block())
        p.replay(block)
    return nc



def emit_l1mid(p, cx, p0, p1, xres2, gw1, gb1, gw2, xres3, xln3, gp,
               msel_h=None):
    t, v, s, g, y = p.t, p.v, p.s, p.g, p.y
    ib = p.sb([128, 8 * 512], F32, "m_ib")
    ib2 = p.sb([128, 8 * 512], F32, "m_ib2")
    ibb = p.sb([128, 8 * 512], BF16, "m_ibb")
    bs = p.sb([128, 8 * 512], F32, "m_bs")
    rs = p.sb([128, 8 * 512], F32, "m_rs")
    rbb = p.sb([128, 8 * 512], BF16, "m_rbb")
    ol = p.sb([128, 8 * 512], F32, "m_ol")
    olb = p.sb([128, 8 * 512], BF16, "m_olb")
    hb = p.sb([128, 8 * 512], BF16, "m_hb")
    tp = p.sb([128, 512], F32, "m_tp")
    gw1_sb = p.sb([128, 8 * D], BF16, "gw1_sb")
    gw2_sb = p.sb([128, 8 * 8], BF16, "gw2_sb")
    gb1_sb = p.sb([128, 8], F32, "gb1_sb")
    gpa = p.sb([8, 1], F32, "gpa")
    gpt = p.sb([8, 1], F32, "gpt")
    if msel_h is not None:
        msel = p.sb([128, 2], F32, "msel_sb")
        ibR = p.sb([128, 8 * 512], F32, "m_ibR")
        ib2R = p.sb([128, 8 * 512], F32, "m_ib2R")
    ps_g = p.ps([128, 512], F32, "ps_g")
    ps_e = p.ps([8, 512], F32, "ps_e")
    lds = p.sem("mld")
    wl = p.sem("mwl")
    for c in range(NCH):
        g.dma_start(out=gw1_sb[:, c * D:(c + 1) * D],
                    in_=gw1[c * 128:(c + 1) * 128, :])
        wl.inc(p.last("gpsimd"), 16)
        g.dma_start(out=gw2_sb[:, c * 8:(c + 1) * 8],
                    in_=gw2[c * 128:(c + 1) * 128, :])
        wl.inc(p.last("gpsimd"), 16)
    y.dma_start(out=gb1_sb[:, :], in_=gb1)
    gb1_ld = lds.inc(p.last("sync"), 16)
    if msel_h is not None:
        y.dma_start(out=msel[:, :], in_=msel_h)
        lds.inc(p.last("sync"), 16)
    v.memset(gpa[:, :], 0.0)
    p.mark("vector")
    wl_t = wl.n
    for tt in range(4):
        y.wait_ge(p.ve.h, p.ve.n)
        y.wait_ge(p.se.h, p.se.n)
        if msel_h is None:
            pairs = ((p0, ib), (p1, ib2), (xres2, bs))
        else:
            pairs = ((p0[:, tt * 512:(tt + 1) * 512 + 0], ib),)
            pairs = (
                (p0[:, tt * 512:(tt + 1) * 512], ib),
                (p1[:, tt * 512:(tt + 1) * 512], ib2),
                (p0[:, HL + tt * 512:HL + (tt + 1) * 512], ibR),
                (p1[:, HL + tt * 512:HL + (tt + 1) * 512], ib2R),
                (xres2[:, tt * 512:(tt + 1) * 512], bs),
            )
        if msel_h is None:
            for src, dst in pairs:
                y.dma_start(out=dst[:, :].rearrange("p (c q) -> p c q", c=NCH),
                            in_=src[:, tt * 512:(tt + 1) * 512].rearrange(
                                "(c p) q -> p c q", p=128))
                lds.inc(p.last("sync"), 16)
        else:
            for src, dst in pairs:
                if src.tensor.dtype == BF16:
                    g.dma_start(out=dst[:, :].rearrange(
                                    "p (c q) -> p c q", c=NCH),
                                in_=src.rearrange("(c p) q -> p c q", p=128))
                    lds.inc(p.last("gpsimd"), 16)
                else:
                    y.dma_start(out=dst[:, :].rearrange(
                                    "p (c q) -> p c q", c=NCH),
                                in_=src.rearrange("(c p) q -> p c q", p=128))
                    lds.inc(p.last("sync"), 16)
        p.wait_all("vector", lds)
        for c in range(NCH):
            sl = slice(c * 512, (c + 1) * 512)
            v.tensor_add(ib[:, sl], ib[:, sl], ib2[:, sl])
            if msel_h is not None:
                v.tensor_add(ibR[:, sl], ibR[:, sl], ib2R[:, sl])
                v.tensor_scalar(ib[:, sl], ib[:, sl], msel[:, 0:1], None,
                                ALU.mult)
                v.scalar_tensor_tensor(ib[:, sl], ibR[:, sl], msel[:, 1:2],
                                       ib[:, sl], ALU.mult, ALU.add)
            v.tensor_copy(ibb[:, sl], ib[:, sl])
        p.mark("vector")
        p.wait_all("scalar", p.ve)
        p.wait_all("tensor", p.ve)
        chunks = [ibb[:, c * 512:(c + 1) * 512] for c in range(NCH)]

        def add_out(c, ch, mb, rb_):
            sl = slice(c * 512, (c + 1) * 512)
            v.tensor_sub(tp[:, :], ib[:, sl], mb)
            v.tensor_mul(tp[:, :], tp[:, :], rb_)
            v.tensor_add(rs[:, sl], tp[:, :], bs[:, sl])
            v.tensor_copy(rbb[:, sl], rs[:, sl])
        cx.ln_block(chunks, 512, add_out)
        y.wait_ge(p.ve.h, p.ve.n)
        y.dma_start(out=xres3[:, tt * 512:(tt + 1) * 512].rearrange(
                        "(c p) q -> p c q", p=128),
                    in_=rs[:, :].rearrange("p (c q) -> p c q", c=NCH))
        p.st.inc(p.last("sync"), 16)
        rbc = [rbb[:, c * 512:(c + 1) * 512] for c in range(NCH)]

        def ln_out(c, ch, mb, rb_):
            sl = slice(c * 512, (c + 1) * 512)
            v.tensor_sub(ol[:, sl], rs[:, sl], mb)
            v.tensor_mul(ol[:, sl], ol[:, sl], rb_)
            v.tensor_copy(olb[:, sl], ol[:, sl])
        cx.ln_block(rbc, 512, ln_out)
        y.wait_ge(p.ve.h, p.ve.n)
        y.dma_start(out=xln3[:, tt * 512:(tt + 1) * 512].rearrange(
                        "(c p) q -> p c q", p=128),
                    in_=ol[:, :].rearrange("p (c q) -> p c q", c=NCH))
        p.st.inc(p.last("sync"), 16)
        p.wait_all("tensor", p.ve)
        if tt == 0:
            t.wait_ge(wl.h, wl_t)
        for oc in range(NCH):
            p.wait_all("tensor", p.se)
            for c in range(NCH):
                t.matmul(ps_g[:, :],
                         gw1_sb[:, c * D + oc * 128:c * D + oc * 128 + 128],
                         olb[:, c * 512:(c + 1) * 512],
                         start=(c == 0), stop=(c == NCH - 1))
            pe_g = p.mark("tensor")
            s.wait_ge(p.pe.h, pe_g)
            s.wait_ge(lds.h, gb1_ld)
            s.activation(hb[:, oc * 512:(oc + 1) * 512], ps_g[:, :],
                         AF.Relu, bias=gb1_sb[:, oc:oc + 1])
            p.mark("scalar")
        p.wait_all("tensor", p.se)
        for c in range(NCH):
            t.matmul(ps_e[:, :], gw2_sb[:, c * 8:(c + 1) * 8],
                     hb[:, c * 512:(c + 1) * 512],
                     start=(c == 0), stop=(c == NCH - 1))
        pe_e = p.mark("tensor")
        v.wait_ge(p.pe.h, pe_e)
        v.reduce_sum(gpt[:, :], ps_e[:, :], axis=AX.X)
        v.tensor_add(gpa[:, :], gpa[:, :], gpt[:, :])
        p.mark("vector")
    y.wait_ge(p.ve.h, p.ve.n)
    y.dma_start(out=gp.rearrange("a b -> b a"), in_=gpa[:, :])
    p.st.inc(p.last("sync"), 16)
    y.wait_ge(p.st.h, p.st.n)



def build_l1mid():
    nc = bass.Bass("TRN2", target_bir_lowering=False)
    p0 = nc.dram_tensor("p0", [D, HL], F32, kind="ExternalInput").ap()
    p1 = nc.dram_tensor("p1", [D, HL], F32, kind="ExternalInput").ap()
    xres2 = nc.dram_tensor("xres2", [D, HL], F32, kind="ExternalInput").ap()
    gw1 = nc.dram_tensor("gw1", [D, D], F32, kind="ExternalInput").ap()
    gb1 = nc.dram_tensor("gb1", [128, 8], F32, kind="ExternalInput").ap()
    gw2 = nc.dram_tensor("gw2", [D, 8], F32, kind="ExternalInput").ap()
    xres3 = nc.dram_tensor("xres3", [D, HL], F32, kind="ExternalOutput").ap()
    xln3 = nc.dram_tensor("xln3", [D, HL], F32, kind="ExternalOutput").ap()
    gp = nc.dram_tensor("gp", [1, 8], F32, kind="ExternalOutput").ap()
    with ExitStack() as stack:
        p = Prog(nc, stack)
        cx = Ctx(p)
        emit_l1mid(p, cx, p0, p1, xres2, gw1, gb1, gw2, xres3, xln3, gp)
        block = stack.enter_context(nc.Block())
        p.replay(block)
    return nc



def emit_l1ffn(p, cx, xln3, xres3, gpa_h, gpb_h, gb2_h, w1t_h, tw1_h,
               b1_h, tb1_h, w2t_h, tw2_h, b2_h, tb2_h, outp, u_hbm, NF1):
    t, v, s, g, y = p.t, p.v, p.s, p.g, p.y
    xnb = p.sb([128, 8 * HL], BF16, "f_xnb")
    grow = p.sb([1, 8], F32, "f_grow")
    g2 = p.sb([1, 8], F32, "f_g2")
    gb2s = p.sb([1, 8], F32, "f_gb2")
    gB = p.sb([128, 8], F32, "f_gB")
    w1s = [p.sb([128, 8 * 128], BF16, f"f_w1s{i}") for i in range(2)]
    tw1s = [p.sb([128, 8 * 128], BF16, f"f_tw1s{i}") for i in range(2)]
    m1 = p.sb([128, 8 * 128], BF16, "f_m1")
    w2s = p.sb([128, NF1 * 128], BF16, "f_w2s")
    tw2s = p.sb([128, NF1 * 128], BF16, "f_tw2s")
    m2f = p.sb([128, 8 * NF1 * 128], BF16, "f_m2f")
    b1s = p.sb([128, 32], F32, "f_b1s")
    tb1s = p.sb([128, 8 * 32], F32, "f_tb1s")
    mb1 = p.sb([128, 32], F32, "f_mb1")
    b2s = p.sb([128, 8], F32, "f_b2s")
    tb2s = p.sb([128, 64], F32, "f_tb2s")
    mb2 = p.sb([128, 8], F32, "f_mb2")
    ut = p.sb([128, HL], BF16, "f_ut")
    uts = xnb[:, 0:NF1 * 512]
    fo = p.sb([128, 8 * 512], BF16, "f_fo")
    xr = p.sb([128, 8 * 512], F32, "f_xr")
    oo = p.sb([128, 8 * 512], F32, "f_oo")
    tp = p.sb([128, 512], F32, "f_tp")
    ps_u = p.ps([128, 512], F32, "ps_u")
    ps_o = p.ps([128, 512], F32, "ps_o")
    ps_b = p.ps([128, 8], F32, "ps_gb")
    lds = p.sem("fld")
    w1l = [p.sem("fw10"), p.sem("fw11")]
    w2l = p.sem("fw2")
    ul = p.sem("ful")

    for src, dst in ((gpa_h, grow[:, :]), (gpb_h, g2[:, :]),
                     (b1_h, b1s[:, :]), (tb1_h, tb1s[:, :]),
                     (b2_h, b2s[:, :]), (tb2_h, tb2s[:, :]),
                     (gb2_h, gb2s[:, :])):
        y.dma_start(out=dst, in_=src)
        lds.inc(p.last("sync"), 16)
    g.dma_start(out=xnb[:, :].rearrange("p (c q) -> p c q", c=NCH),
                in_=xln3.rearrange("(c p) q -> p c q", p=128))
    lds.inc(p.last("gpsimd"), 16)
    p.wait_all("vector", lds)
    v.tensor_add(grow[:, :], grow[:, :], g2[:, :])
    v.tensor_single_scalar(grow[:, :], grow[:, :], 1.0 / L, ALU.mult)
    v.tensor_add(grow[:, :], grow[:, :], gb2s[:, :])
    ve_g = p.mark("vector")
    t.wait_ge(p.ve.h, ve_g)
    t.matmul(ps_b[:, :], cx.ones1_f[:, :], grow[:, :], start=True,
             stop=True)
    pe_b = p.mark("tensor")
    v.wait_ge(p.pe.h, pe_b)
    v.tensor_copy(gB[:, :], ps_b[:, :])
    v.tensor_copy(mb1[:, :], b1s[:, :])
    for e in range(8):
        v.scalar_tensor_tensor(mb1[:, :], tb1s[:, e * 32:(e + 1) * 32],
                               gB[:, e:e + 1], mb1[:, :], ALU.mult,
                               ALU.add)
    v.tensor_copy(mb2[:, :], b2s[:, :])
    for e in range(8):
        v.scalar_tensor_tensor(mb2[:, :], tb2s[:, e * 8:(e + 1) * 8],
                               gB[:, e:e + 1], mb2[:, :], ALU.mult,
                               ALU.add)
    p.mark("vector")

    w1c = [0, 0]
    for fc in range(NF1):
        sl = fc % 2
        if fc >= 2:
            g.wait_ge(p.ve.h, w1c[sl])
        g.dma_start(out=w1s[sl][:, :].rearrange("p (c f) -> p c f", c=NCH),
                    in_=w1t_h[:, fc * 128:(fc + 1) * 128].rearrange(
                        "(c p) f -> p c f", p=128))
        w1l[sl].inc(p.last("gpsimd"), 16)
        for e in range(8):
            g.dma_start(
                out=tw1s[sl][:, :].rearrange("p (c f) -> p c f", c=NCH),
                in_=tw1_h[e, :, fc * 128:(fc + 1) * 128].rearrange(
                    "(c p) f -> p c f", p=128))
            w1l[sl].inc(p.last("gpsimd"), 16)
            p.wait_all("vector", w1l[sl])
            if e == 0:
                v.wait_ge(p.pe.h, p.pe.n)
                v.tensor_copy(m1[:, :], w1s[sl][:, :])
            v.scalar_tensor_tensor(m1[:, :], tw1s[sl][:, :],
                                   gB[:, e:e + 1], m1[:, :], ALU.mult,
                                   ALU.add)
            ve_m1 = p.mark("vector")
            if e < 7:
                g.wait_ge(p.ve.h, ve_m1)
        w1c[sl] = p.ve.n
        p.wait_all("tensor", p.ve)
        for half in range(4):
            t.wait_ge(p.se.h, p.se.n)
            for c in range(NCH):
                t.matmul(ps_u[:, :], m1[:, c * 128:(c + 1) * 128],
                         xnb[:, c * HL + half * 512:
                             c * HL + half * 512 + 512],
                         start=(c == 0), stop=(c == NCH - 1))
            pe_u = p.mark("tensor")
            s.wait_ge(p.pe.h, pe_u)
            s.wait_ge(p.st.h, p.st.n)
            s.activation(ut[:, half * 512:half * 512 + 512], ps_u[:, :],
                         AF.Gelu, bias=mb1[:, fc:fc + 1])
            p.mark("scalar")
        se_u = p.se.n
        y.wait_ge(p.se.h, se_u)
        y.dma_start(out=u_hbm[fc, :, :], in_=ut[:, :])
        p.st.inc(p.last("sync"), 16)

    for oc in range(NCH):
        g.wait_ge(p.ve.h, p.ve.n)
        g.dma_start(out=w2s[:, :].rearrange("p (f o) -> p f o", f=NF1),
                    in_=w2t_h[:, oc * 128:(oc + 1) * 128].rearrange(
                        "(f p) o -> p f o", p=128))
        w2l.inc(p.last("gpsimd"), 16)
        for e in range(8):
            g.dma_start(
                out=tw2s[:, :].rearrange("p (f o) -> p f o", f=NF1),
                in_=tw2_h[e, :, oc * 128:(oc + 1) * 128].rearrange(
                    "(f p) o -> p f o", p=128))
            w2l.inc(p.last("gpsimd"), 16)
            p.wait_all("vector", w2l)
            dst = m2f[:, oc * NF1 * 128:(oc + 1) * NF1 * 128]
            if e == 0:
                v.tensor_copy(dst, w2s[:, :])
            v.scalar_tensor_tensor(dst, tw2s[:, :], gB[:, e:e + 1],
                                   dst, ALU.mult, ALU.add)
            ve_m2 = p.mark("vector")
            g.wait_ge(p.ve.h, ve_m2)

    for tt in range(4):
        y.wait_ge(p.st.h, p.st.n)
        y.wait_ge(p.pe.h, p.pe.n)
        y.wait_ge(p.ve.h, p.ve.n)
        y.dma_start(out=uts[:, :].rearrange("p (f q) -> p f q", f=NF1),
                    in_=u_hbm[:, :, tt * 512:(tt + 1) * 512].rearrange(
                        "f p q -> p f q"))
        ul.inc(p.last("sync"), 16)
        y.dma_start(out=xr[:, :].rearrange("p (c q) -> p c q", c=NCH),
                    in_=xres3[:, tt * 512:(tt + 1) * 512].rearrange(
                        "(c p) q -> p c q", p=128))
        ul.inc(p.last("sync"), 16)
        p.wait_all("tensor", ul)
        for oc in range(NCH):
            p.wait_all("tensor", p.ve)
            for fcc in range(NF1):
                t.matmul(ps_o[:, :],
                         m2f[:, (oc * NF1 + fcc) * 128:
                             (oc * NF1 + fcc) * 128 + 128],
                         uts[:, fcc * 512:(fcc + 1) * 512],
                         start=(fcc == 0), stop=(fcc == NF1 - 1))
            pe_o = p.mark("tensor")
            v.wait_ge(p.pe.h, pe_o)
            v.tensor_scalar(fo[:, oc * 512:(oc + 1) * 512], ps_o[:, :],
                            mb2[:, oc:oc + 1], None, ALU.add)
            p.mark("vector")
        foc = [fo[:, c * 512:(c + 1) * 512] for c in range(NCH)]

        def out_fn(c, ch, mb, rb_):
            sl2 = slice(c * 512, (c + 1) * 512)
            v.tensor_sub(tp[:, :], ch, mb)
            v.tensor_mul(tp[:, :], tp[:, :], rb_)
            v.tensor_add(oo[:, sl2], tp[:, :], xr[:, sl2])
        cx.ln_block(foc, 512, out_fn)
        y.wait_ge(p.ve.h, p.ve.n)
        y.dma_start(out=outp[:, tt * 512:(tt + 1) * 512].rearrange(
                        "(c p) q -> p c q", p=128),
                    in_=oo[:, :].rearrange("p (c q) -> p c q", c=NCH))
        p.st.inc(p.last("sync"), 16)
    y.wait_ge(p.st.h, p.st.n)



def build_l1ffn():
    nc = bass.Bass("TRN2", target_bir_lowering=False)
    xln3 = nc.dram_tensor("xln3", [D, HL], F32, kind="ExternalInput").ap()
    xres3 = nc.dram_tensor("xres3", [D, HL], F32, kind="ExternalInput").ap()
    gpa_h = nc.dram_tensor("gpa", [1, 8], F32, kind="ExternalInput").ap()
    gpb_h = nc.dram_tensor("gpb", [1, 8], F32, kind="ExternalInput").ap()
    gb2_h = nc.dram_tensor("gb2", [1, 8], F32, kind="ExternalInput").ap()
    w1t_h = nc.dram_tensor("w1t", [D, 4096], F32, kind="ExternalInput").ap()
    tw1_h = nc.dram_tensor("tw1", [8, D, 4096], F32, kind="ExternalInput").ap()
    b1_h = nc.dram_tensor("b1m", [128, 32], F32, kind="ExternalInput").ap()
    tb1_h = nc.dram_tensor("tb1m", [128, 8 * 32], F32, kind="ExternalInput").ap()
    w2t_h = nc.dram_tensor("w2t", [4096, D], F32, kind="ExternalInput").ap()
    tw2_h = nc.dram_tensor("tw2", [8, 4096, D], F32, kind="ExternalInput").ap()
    b2_h = nc.dram_tensor("b2m", [128, 8], F32, kind="ExternalInput").ap()
    tb2_h = nc.dram_tensor("tb2m", [128, 8 * 8], F32, kind="ExternalInput").ap()
    outp = nc.dram_tensor("outp", [D, HL], F32, kind="ExternalOutput").ap()
    u_hbm = nc.dram_tensor("u_hbm", [32, 128, HL], BF16).ap()
    with ExitStack() as stack:
        p = Prog(nc, stack)
        cx = Ctx(p)
        emit_l1ffn(p, cx, xln3, xres3, gpa_h, gpb_h, gb2_h, w1t_h, tw1_h,
                   b1_h, tb1_h, w2t_h, tw2_h, b2_h, tb2_h, outp, u_hbm,
                   4096 // 128)
        block = stack.enter_context(nc.Block())
        p.replay(block)
    return nc



def stage_l1mid(inputs, pouts, xres2_full):
    gw1 = np.asarray(inputs["L1_ffn_gate_w1"])
    gb1 = np.asarray(inputs["L1_ffn_gate_b1"])
    gw2 = np.asarray(inputs["L1_ffn_gate_w2"])
    maps = []
    for c in range(NCORES):
        b, j = divmod(c, 2)
        sl = slice(j * HL, (j + 1) * HL)
        maps.append(dict(
            p0=np.ascontiguousarray(pouts[2 * b][:, sl]),
            p1=np.ascontiguousarray(pouts[2 * b + 1][:, sl]),
            xres2=np.ascontiguousarray(xres2_full[b, sl].T),
            gw1=np.ascontiguousarray(gw1.T), gb1=_cm(gb1),
            gw2=np.ascontiguousarray(gw2.T),
        ))
    return maps


def stage_l1ffn(inputs, gps, xres3s, xln3s):
    W1 = np.asarray(inputs["L1_ffn_W1"])
    b1 = np.asarray(inputs["L1_ffn_b1"])
    W2 = np.asarray(inputs["L1_ffn_W2"])
    b2 = np.asarray(inputs["L1_ffn_b2"])
    TW1 = np.asarray(inputs["L1_ffn_TW1"])
    Tb1 = np.asarray(inputs["L1_ffn_Tb1"])
    TW2 = np.asarray(inputs["L1_ffn_TW2"])
    Tb2 = np.asarray(inputs["L1_ffn_Tb2"])
    gb2 = np.asarray(inputs["L1_ffn_gate_b2"])
    w1t = np.ascontiguousarray(W1.T)
    tw1 = np.ascontiguousarray(np.transpose(TW1, (0, 2, 1)))
    w2t = np.ascontiguousarray(W2.T)
    tw2 = np.ascontiguousarray(np.transpose(TW2, (0, 2, 1)))
    b1m = np.ascontiguousarray(b1.reshape(32, 128).T)
    tb1m = np.zeros((128, 8 * 32), dtype=np.float32)
    for e in range(8):
        tb1m[:, e * 32:(e + 1) * 32] = Tb1[e].reshape(32, 128).T
    b2m = _cm(b2)
    tb2m = np.zeros((128, 64), dtype=np.float32)
    for e in range(8):
        tb2m[:, e * 8:(e + 1) * 8] = Tb2[e].reshape(8, 128).T
    maps = []
    for c in range(NCORES):
        b, j = divmod(c, 2)
        maps.append(dict(
            xln3=xln3s[c], xres3=xres3s[c],
            gpa=gps[2 * b], gpb=gps[2 * b + 1],
            gb2=gb2.reshape(1, 8).astype(np.float32),
            w1t=w1t, tw1=tw1, b1m=b1m, tb1m=tb1m,
            w2t=w2t, tw2=tw2, b2m=b2m, tb2m=tb2m,
        ))
    return maps



def _asm_full(res_list, key):
    full = np.zeros((B, L, D), dtype=np.float32)
    for c in range(NCORES):
        b, j = divmod(c, 2)
        full[b, j * HL:(j + 1) * HL] = res_list[c][key].T
    return full


_PROGS = {}


def _prog(name, fn):
    if name not in _PROGS:
        _PROGS[name] = fn()
    return _PROGS[name]


def kernel(**inputs):
    inputs = {k: np.asarray(v) for k, v in inputs.items()}
    cores = list(range(NCORES))

    maps = stage_l0_attn_inputs(inputs)
    r0 = run_bass_kernel_spmd(_prog("l0a0", lambda: build_l0_attn(0)),
                              [maps[2 * b] for b in range(B)],
                              core_ids=[0, 2, 4, 6])
    r1 = run_bass_kernel_spmd(_prog("l0a1", lambda: build_l0_attn(1)),
                              [maps[2 * b + 1] for b in range(B)],
                              core_ids=[1, 3, 5, 7])
    res = [None] * NCORES
    for b in range(B):
        res[2 * b] = r0.results[b]
        res[2 * b + 1] = r1.results[b]
    xres1 = [res[c]["xres1"] for c in range(NCORES)]
    xin_full = _asm_full(res, "xin0")

    for layer in range(6):
        m = stage_l0f_attn(inputs, layer, xin_full)
        ra = run_bass_kernel_spmd(_prog("l0fa", build_l0f_attn), m,
                                  core_ids=cores)
        xp_full = _asm_full(ra.results, "xout")
        m = stage_l0f_ffn(inputs, layer, xp_full)
        rf = run_bass_kernel_spmd(_prog("l0ff", build_l0f_ffn), m,
                                  core_ids=cores)
        xin_full = _asm_full(rf.results, "xout")

    m = [dict(xres1=xres1[c],
              xin6=np.ascontiguousarray(
                  xin_full[divmod(c, 2)[0],
                           divmod(c, 2)[1] * HL:(divmod(c, 2)[1] + 1) * HL].T))
         for c in range(NCORES)]
    re_ = run_bass_kernel_spmd(_prog("l0epi", build_l0epi), m, core_ids=cores)
    xres2_full = _asm_full(re_.results, "xres2")
    xln2_full = _asm_full(re_.results, "xln2")

    m = stage_l1_attn(inputs, xln2_full)
    r4 = run_bass_kernel_spmd(_prog("l1a", build_l1_attn), m, core_ids=cores)
    pouts = [r4.results[c]["pout"] for c in range(NCORES)]

    m = stage_l1mid(inputs, pouts, xres2_full)
    r5 = run_bass_kernel_spmd(_prog("l1m", build_l1mid), m, core_ids=cores)
    gps = [r5.results[c]["gp"] for c in range(NCORES)]
    xres3s = [r5.results[c]["xres3"] for c in range(NCORES)]
    xln3s = [r5.results[c]["xln3"] for c in range(NCORES)]

    m = stage_l1ffn(inputs, gps, xres3s, xln3s)
    r6 = run_bass_kernel_spmd(_prog("l1f", build_l1ffn), m, core_ids=cores)
    return _asm_full(r6.results, "outp")



PAIRS = [[0, 1], [2, 3], [4, 5], [6, 7]]


def build_fused(nocc=False, nlayers=6):
    nc = bass.Bass("TRN2", target_bir_lowering=False)
    ap = lambda name, shape, dt=F32: nc.dram_tensor(
        name, shape, dt, kind="ExternalInput").ap()

    x_arr = ap("x_arr", [D, L])
    wq_h, wk_h = ap("wq", [D, D]), ap("wk", [D, D])
    wv_h, wo_h = ap("wv", [D, D]), ap("wo", [D, D])
    dmask_h = ap("dmask", [128, 128])
    krow_h = ap("krow", [1, L])
    wqkv6 = ap("wqkv6", [6, D, 3 * D])
    wo6 = ap("wo6", [6, D, D])
    qs6, ks6 = ap("qs6", [6, 128, 1]), ap("ks6", [6, 128, 1])
    tq1_h, tq2_h = ap("tq1", [128, HB + HL]), ap("tq2", [128, HB + HL])
    tk1_h, tk2_h = ap("tk1", [128, HB + HL]), ap("tk2", [128, HB + HL])
    rt_h = ap("rt", [128, 128])
    hm_h, hmt_h = ap("hm", [8, 128, 16]), ap("hmt", [8, 16, 128])
    oh_h = ap("oh", [16, 16 * 128])
    amask_h = ap("amask", [2, 4, 128, W])
    hscale_h = ap("hscale", [128, 1])
    w1p6 = ap("w1p6", [6, D, 2 * FPAD])
    b1m6 = ap("b1m6", [6, 128, 44])
    w2p6 = ap("w2p6", [6, FPAD, D])
    b2m6 = ap("b2m6", [6, 128, 8])
    lng6, lnb6 = ap("lng6", [6, 128, 8]), ap("lnb6", [6, 128, 8])
    winx_h, winz_h = ap("winx", [D, D]), ap("winz", [D, D])
    convw_h, convb_h = ap("convw", [128, 32]), ap("convb", [128, 8])
    ldt_h = ap("ldt", [128, 8])
    lar_h, aim_h = ap("lar", [128, 64]), ap("aim", [128, 64])
    cre_h, cim_h = ap("cre", [128, 64]), ap("cim", [128, 64])
    dvec_h = ap("dvec", [128, 8])
    wout_h = ap("wout", [D, D])
    gw1_h = ap("gw1m", [D, D])
    gb1_h = ap("gb1m", [128, 8])
    gw2_h = ap("gw2m", [D, 8])
    w1t_h = ap("w1t", [D, 4096])
    tw1_h = ap("tw1", [8, D, 4096])
    b1f_h = ap("b1m", [128, 32])
    tb1_h = ap("tb1m", [128, 8 * 32])
    w2t_h = ap("w2t", [4096, D])
    tw2_h = ap("tw2", [8, 4096, D])
    b2f_h = ap("b2m", [128, 8])
    tb2_h = ap("tb2m", [128, 8 * 8])
    gb2_h = ap("gb2", [1, 8])
    zrow_h = ap("zrow", [1, 8])
    joff_h = nc.dram_tensor("joff", [1, 8], I32, kind="ExternalInput").ap()
    bsel_h = ap("bsel", [8, 1])
    msel_h = None

    outp = nc.dram_tensor("outp", [D, HL], F32, kind="ExternalOutput").ap()

    khbm = nc.dram_tensor("khbm", [NH, 65, L], BF16).ap()
    xres1_d = nc.dram_tensor("xres1_d", [D, HL], F32).ap()
    xcat = [nc.dram_tensor(f"xcat{i}", [D, HB + HL], F32).ap()
            for i in range(2)]
    xpcat = nc.dram_tensor("xpcat", [D, 4 + HL], F32).ap()
    ag1in = nc.dram_tensor("ag1in", [D, HB], F32).ap()
    ag1out_t = nc.dram_tensor("ag1out", [8 * D, HB], F32)
    ag1out = ag1out_t.ap()
    halo1_d = nc.dram_tensor("halo1_d", [D, HB], F32).ap()
    ag2in = nc.dram_tensor("ag2in", [D, 4], F32).ap()
    ag2out_t = nc.dram_tensor("ag2out", [8 * D, 4], F32)
    ag2out = ag2out_t.ap()
    halo2_d = nc.dram_tensor("halo2_d", [D, 4], F32).ap()
    xres2_d = nc.dram_tensor("xres2_d", [D, HL], F32).ap()
    xln2_d = nc.dram_tensor("xln2_d", [D, HL], F32).ap()
    xln2bf_d = nc.dram_tensor("xln2bf_d", [D, HL], BF16).ap()
    agxg_t = nc.dram_tensor("agxg", [8 * D, HL], BF16)
    agxg = agxg_t.ap()
    xg_d = nc.dram_tensor("xg_d", [D, L], BF16).ap()
    pout_d = nc.dram_tensor("pout_d", [D, L], BF16).ap()
    agpo_t = nc.dram_tensor("agpo", [8 * D, L], BF16)
    agpo = agpo_t.ap()
    p0_d = nc.dram_tensor("p0_d", [D, HL], F32).ap()
    p1_d = nc.dram_tensor("p1_d", [D, HL], F32).ap()
    aggp_d = nc.dram_tensor("aggp_d", [8, 8], F32).ap()
    gp_d = nc.dram_tensor("gp_d", [1, 8], F32).ap()
    argp_d = nc.dram_tensor("argp_d", [1, 8], F32).ap()
    xres3_d = nc.dram_tensor("xres3_d", [D, HL], F32).ap()
    xln3_d = nc.dram_tensor("xln3_d", [D, HL], F32).ap()
    u_hbm = nc.dram_tensor("u_hbm", [32, 128, HL], BF16).ap()

    with ExitStack() as stack:
        p = Prog(nc, stack)
        cx = Ctx(p)
        ccs = p.sem("ccs")
        p.ccs = ccs
        t, v, s, g, y = p.t, p.v, p.s, p.g, p.y
        jreg = stack.enter_context(nc.gpsimd.register("jreg"))
        hsc = p.sb([128, 1], F32, "hsc_sb")
        joff_sb = p.sb([1, 8], I32, "joff_sb")
        bsel_sb = p.sb([8, 1], F32, "bsel_sb")
        gp8 = p.sb([8, 8], F32, "gp8_sb")
        gprow = p.sb([1, 8], F32, "gprow_sb")
        ps_gsel = p.ps([1, 8], F32, "ps_gsel")
        y.dma_start(out=hsc[:, :], in_=hscale_h)
        p.ld.inc(p.last("sync"), 16)
        y.dma_start(out=joff_sb[:, :], in_=joff_h)
        jl_ld = p.ld.inc(p.last("sync"), 16)

        def dyncopy(dst_ap, src_handle, dims, k):
            g.wait_ge(p.ld.h, jl_ld)
            g.reg_load(jreg, joff_sb[0:1, k:k + 1])
            g.dma_start(out=dst_ap, in_=bass.AP(src_handle, jreg, dims))
            p.st.inc(p.last("gpsimd"), 16)
            p.wait_all("sync", p.st)
            p.wait_all("gpsimd", p.st)

        GRP = [list(range(8))]

        def collective(kind, op, in_ap, out_ap):
            if nocc:
                y.dma_start(out=out_ap[0:in_ap.shape[0], :], in_=in_ap)
                p.st.inc(p.last("sync"), 16)
                p.wait_all("gpsimd", p.st)
                return
            rc = g.collective_compute(kind, op, replica_groups=GRP,
                                      ins=[in_ap], outs=[out_ap])
            ccs.inc(rc, 1)
            p.wait_all("gpsimd", ccs)

        def halo_fix(src_ap, dst_ap, ncols):
            p.push_scope()
            halo = p.sb([128, 8 * HB], F32, "halo_sb")
            p.wait_all("sync", ccs)
            y.dma_start(out=halo[:, :].rearrange(
                            "p (c q) -> p c q", c=NCH)[:, :, 0:ncols],
                        in_=src_ap[0:D, 0:ncols].rearrange(
                            "(c p) q -> p c q", p=128))
            hl_ = p.ld.inc(p.last("sync"), 16)
            v.wait_ge(p.ld.h, hl_)
            hv = halo[:, :].rearrange("p (c q) -> p c q", c=NCH)[:, :, 0:ncols]
            v.tensor_scalar(hv, hv, hsc[:, 0:1], None, ALU.mult)
            ve_h = p.mark("vector")
            y.wait_ge(p.ve.h, ve_h)
            y.dma_start(out=dst_ap[:, 0:ncols].rearrange(
                            "(c p) q -> p c q", p=128),
                        in_=halo[:, :].rearrange(
                            "p (c q) -> p c q", c=NCH)[:, :, 0:ncols])
            p.st.inc(p.last("sync"), 16)
            p.wait_all("gpsimd", p.st)
            p.stage_barrier()
            p.pop_scope()

        p.push_scope()
        emit_l0_attn(p, cx, 1, x_arr, wq_h, wk_h, wv_h, wo_h, dmask_h,
                     xres1_d, xcat[0][:, HB:HB + HL], khbm, krow_h=krow_h)
        p.stage_barrier()
        p.pop_scope()

        for layer in range(nlayers):
            cur = xcat[layer % 2]
            nxt = xcat[(layer + 1) % 2]
            y.dma_start(out=ag1in, in_=cur[:, HL:HB + HL])
            p.st.inc(p.last("sync"), 16)
            p.wait_all("gpsimd", p.st)
            collective("AllGather", ALU.bypass, ag1in, ag1out)
            p.stage_barrier()
            dyncopy(halo1_d, ag1out_t, [[HB, D], [1, HB]], 0)
            p.stage_barrier()
            halo_fix(halo1_d, cur, HB)
            p.stage_barrier()
            p.push_scope()
            emit_l0f_attn(p, cx, cur, wqkv6[layer], wo6[layer], qs6[layer],
                          ks6[layer], tq1_h, tq2_h, tk1_h, tk2_h, rt_h,
                          hm_h, hmt_h, oh_h, amask_h, xpcat[:, 4:4 + HL])
            p.stage_barrier()
            p.pop_scope()
            y.dma_start(out=ag2in, in_=xpcat[:, HL:HL + 4])
            p.st.inc(p.last("sync"), 16)
            p.wait_all("gpsimd", p.st)
            collective("AllGather", ALU.bypass, ag2in, ag2out)
            p.stage_barrier()
            dyncopy(halo2_d, ag2out_t, [[4, D], [1, 4]], 1)
            p.stage_barrier()
            halo_fix(halo2_d, xpcat, 4)
            p.stage_barrier()
            p.push_scope()
            emit_l0f_ffn(p, cx, xpcat, w1p6[layer], b1m6[layer], w2p6[layer],
                         b2m6[layer], lng6[layer], lnb6[layer],
                         nxt[:, HB:HB + HL])
            p.stage_barrier()
            p.pop_scope()

        fin = xcat[nlayers % 2]
        p.push_scope()
        emit_sandwich_add(p, cx, xres1_d, fin[:, HB:HB + HL], xres2_d,
                          xln2_d, out_ln_bf_h=xln2bf_d)
        p.stage_barrier()
        p.pop_scope()

        collective("AllGather", ALU.bypass, xln2bf_d, agxg)
        p.stage_barrier()
        dyncopy(xg_d[:, 0:HL], agxg_t, [[HL, D], [1, HL]], 2)
        dyncopy(xg_d[:, HL:L], agxg_t, [[HL, D], [1, HL]], 3)
        p.stage_barrier()
        p.push_scope()

        def xg_tile(tt):
            return xg_d[:, tt * 512:(tt + 1) * 512]

        def pout_ap(dc, tt):
            return pout_d[dc * 128:(dc + 1) * 128,
                          tt * 512:(tt + 1) * 512]
        emit_l1_attn(p, cx, None, winx_h, winz_h, convw_h, convb_h, ldt_h,
                     lar_h, aim_h, cre_h, cim_h, dvec_h, wout_h, None,
                     xg_tile=xg_tile, pout_ap=pout_ap)
        p.stage_barrier()
        p.pop_scope()

        collective("AllGather", ALU.bypass, pout_d, agpo)
        p.stage_barrier()
        dyncopy(p0_d, agpo_t, [[L, D], [1, HL]], 4)
        dyncopy(p1_d, agpo_t, [[L, D], [1, HL]], 5)
        p.stage_barrier()
        p.push_scope()
        emit_l1mid(p, cx, p0_d, p1_d, xres2_d,
                   gw1_h, gb1_h, gw2_h, xres3_d, xln3_d, gp_d)
        p.stage_barrier()
        p.pop_scope()

        collective("AllGather", ALU.bypass, gp_d, aggp_d)
        p.stage_barrier()
        y.dma_start(out=gp8[:, :], in_=aggp_d)
        gp_ld = p.ld.inc(p.last("sync"), 16)
        y.dma_start(out=bsel_sb[:, :], in_=bsel_h)
        gp_ld2 = p.ld.inc(p.last("sync"), 16)
        t.wait_ge(p.ld.h, gp_ld2)
        t.matmul(ps_gsel[:, :], bsel_sb[:, :], gp8[:, :], start=True,
                 stop=True)
        pe_gs = p.mark("tensor")
        v.wait_ge(p.pe.h, pe_gs)
        v.tensor_copy(gprow[:, :], ps_gsel[:, :])
        ve_gs = p.mark("vector")
        y.wait_ge(p.ve.h, ve_gs)
        y.dma_start(out=argp_d, in_=gprow[:, :])
        p.st.inc(p.last("sync"), 16)
        p.stage_barrier()
        p.push_scope()
        emit_l1ffn(p, cx, xln3_d, xres3_d, argp_d, zrow_h, gb2_h, w1t_h,
                   tw1_h, b1f_h, tb1_h, w2t_h, tw2_h, b2f_h, tb2_h, outp,
                   u_hbm, 4096 // 128)
        p.stage_barrier()
        p.pop_scope()
        y.wait_ge(p.st.h, p.st.n)

        from concourse.aot_env import allow_ici_all_to_all
        block = stack.enter_context(nc.Block())
        with allow_ici_all_to_all():
            p.replay(block)
    return nc


def stage_fused(inputs):
    x = np.asarray(inputs["L0_attn_x"])
    dm = _dmask_np()
    qs6 = np.stack([np.tile(np.asarray(inputs["L0_ffn_q_scale"][i]), 2)
                    .reshape(128, 1) for i in range(6)]).astype(np.float32)
    ks6 = np.stack([np.tile(np.asarray(inputs["L0_ffn_k_scale"][i]), 2)
                    .reshape(128, 1) for i in range(6)]).astype(np.float32)
    DI = 2730
    w1p6 = np.zeros((6, D, 2 * FPAD), dtype=np.float32)
    b1m6 = np.zeros((6, 128, 44), dtype=np.float32)
    w2p6 = np.zeros((6, FPAD, D), dtype=np.float32)
    for i in range(6):
        W1 = np.asarray(inputs["L0_ffn_W1"][i])
        b1 = np.asarray(inputs["L0_ffn_b1"][i])
        w1p6[i, :, 0:DI] = W1[:, 0:DI]
        w1p6[i, :, FPAD:FPAD + DI] = W1[:, DI:]
        b1p = np.zeros((2 * FPAD,), dtype=np.float32)
        b1p[0:DI] = b1[0:DI]
        b1p[FPAD:FPAD + DI] = b1[DI:]
        b1m6[i] = b1p.reshape(44, 128).T
        w2p6[i, 0:DI] = np.asarray(inputs["L0_ffn_W2"][i])
    b2m6 = np.stack([_cm(inputs["L0_ffn_b2"][i]) for i in range(6)])
    lng6 = np.stack([_cm(inputs["L0_ffn_ln_g"][i]) for i in range(6)])
    lnb6 = np.stack([_cm(inputs["L0_ffn_ln_b"][i]) for i in range(6)])
    W_in = np.asarray(inputs["L1_attn_W_in"])
    W_out = np.asarray(inputs["L1_attn_W_out"])
    cw = np.asarray(inputs["L1_attn_conv_w"])
    cb = np.asarray(inputs["L1_attn_conv_b"])
    H2 = 2048
    gw1 = np.asarray(inputs["L1_ffn_gate_w1"])
    gb1 = np.asarray(inputs["L1_ffn_gate_b1"])
    gw2 = np.asarray(inputs["L1_ffn_gate_w2"])
    W1f = np.asarray(inputs["L1_ffn_W1"])
    b1f = np.asarray(inputs["L1_ffn_b1"])
    W2f = np.asarray(inputs["L1_ffn_W2"])
    b2f = np.asarray(inputs["L1_ffn_b2"])
    TW1 = np.asarray(inputs["L1_ffn_TW1"])
    Tb1 = np.asarray(inputs["L1_ffn_Tb1"])
    TW2 = np.asarray(inputs["L1_ffn_TW2"])
    Tb2 = np.asarray(inputs["L1_ffn_Tb2"])
    gb2 = np.asarray(inputs["L1_ffn_gate_b2"])
    w1t = np.ascontiguousarray(W1f.T)
    tw1 = np.ascontiguousarray(np.transpose(TW1, (0, 2, 1)))
    w2t = np.ascontiguousarray(W2f.T)
    tw2 = np.ascontiguousarray(np.transpose(TW2, (0, 2, 1)))
    b1mf = np.ascontiguousarray(b1f.reshape(32, 128).T)
    tb1m = np.zeros((128, 8 * 32), dtype=np.float32)
    for e in range(8):
        tb1m[:, e * 32:(e + 1) * 32] = Tb1[e].reshape(32, 128).T
    b2mf = _cm(b2f)
    tb2m = np.zeros((128, 64), dtype=np.float32)
    for e in range(8):
        tb2m[:, e * 8:(e + 1) * 8] = Tb2[e].reshape(8, 128).T
    zrow = np.zeros((1, 8), dtype=np.float32)

    maps = []
    for c in range(NCORES):
        b, j = divmod(c, 2)
        if j == 1:
            x_arr = np.ascontiguousarray(x[b].T)
            krow = np.ones((1, L), dtype=np.float32)
        else:
            x_arr = np.zeros((D, L), dtype=np.float32)
            x_arr[:, HL:] = x[b, 0:HL].T
            krow = np.ones((1, L), dtype=np.float32)
            krow[0, 0:HL] = 30000.0
        tq1, tq2, tk1, tk2 = rope_tables(j)
        rt, hm, hmt, oh, amask = l0f_consts(j)
        hscale = np.full((128, 1), float(j), dtype=np.float32)
        msel = np.zeros((128, 2), dtype=np.float32)
        msel[:, 0] = 1.0 - j
        msel[:, 1] = float(j)
        pred = c - 1 if j == 1 else c
        joff = np.zeros((1, 8), dtype=np.int32)
        joff[0, 0] = pred * (1024 * HB)
        joff[0, 1] = pred * (1024 * 4)
        joff[0, 2] = (2 * b) * (1024 * HL)
        joff[0, 3] = (2 * b + 1) * (1024 * HL)
        joff[0, 4] = (2 * b) * (1024 * L) + j * HL
        joff[0, 5] = (2 * b + 1) * (1024 * L) + j * HL
        bsel = np.zeros((8, 1), dtype=np.float32)
        bsel[2 * b, 0] = 1.0
        bsel[2 * b + 1, 0] = 1.0
        ch0 = j * 1024
        rows = slice(ch0, ch0 + 1024)
        zrows = slice(H2 + ch0, H2 + ch0 + 1024)
        maps.append(dict(
            x_arr=x_arr,
            wq=np.asarray(inputs["L0_attn_Wq"]),
            wk=np.asarray(inputs["L0_attn_Wk"]),
            wv=np.asarray(inputs["L0_attn_Wv"]),
            wo=np.asarray(inputs["L0_attn_Wo"]),
            dmask=dm, krow=krow,
            wqkv6=np.asarray(inputs["L0_ffn_Wqkv"]),
            wo6=np.asarray(inputs["L0_ffn_Wo"]),
            qs6=qs6, ks6=ks6,
            tq1=tq1, tq2=tq2, tk1=tk1, tk2=tk2,
            rt=rt, hm=hm, hmt=hmt, oh=oh, amask=amask, hscale=hscale,
            w1p6=w1p6, b1m6=b1m6, w2p6=w2p6, b2m6=b2m6,
            lng6=lng6, lnb6=lnb6,
            winx=np.ascontiguousarray(W_in[rows].T),
            winz=np.ascontiguousarray(W_in[zrows].T),
            convw=_cm2(cw[rows]), convb=_cm(cb[rows]),
            ldt=_cm(inputs["L1_attn_log_dt"][rows]),
            lar=_cm2(np.asarray(inputs["L1_attn_log_A_real"])[rows]),
            aim=_cm2(np.asarray(inputs["L1_attn_A_imag"])[rows]),
            cre=_cm2(np.asarray(inputs["L1_attn_C_re"])[rows]),
            cim=_cm2(np.asarray(inputs["L1_attn_C_im"])[rows]),
            dvec=_cm(np.asarray(inputs["L1_attn_D"])[rows]),
            wout=np.ascontiguousarray(W_out[:, rows].T),
            gw1m=np.ascontiguousarray(gw1.T), gb1m=_cm(gb1),
            gw2m=np.ascontiguousarray(gw2.T),
            w1t=w1t, tw1=tw1, b1m=b1mf, tb1m=tb1m,
            w2t=w2t, tw2=tw2, b2m=b2mf, tb2m=tb2m,
            gb2=gb2.reshape(1, 8).astype(np.float32), zrow=zrow,
            joff=joff, bsel=bsel,
        ))
    return maps


def kernel(**inputs):
    inputs = {k: np.asarray(v) for k, v in inputs.items()}
    maps = stage_fused(inputs)
    nc = _prog("fused", build_fused)
    r = run_bass_kernel_spmd(nc, maps, core_ids=list(range(NCORES)))
    return _asm_full(r.results, "outp")

